# version 01
# baseline (speedup 1.0000x reference)
import numpy as np
import ml_dtypes
import concourse.bass as bass
import concourse.mybir as mybir
from concourse.bass_utils import run_bass_kernel_spmd
from contextlib import ExitStack

F32 = mybir.dt.float32
BF16 = mybir.dt.bfloat16
AF = mybir.ActivationFunctionType
ALU = mybir.AluOpType
AX = mybir.AxisListType

DM = 1024
NCH = 8
S = 2048
NB = 2
T = NB * S
H = 16
DH = 64
TC = 512
NTC_B = S // TC
NTC = T // TC
DI = 2048
DS = 16
NEXP = 8
HF = 2048

BF16NP = ml_dtypes.bfloat16
_CACHE = {}

SP, PE, VE, SE, PO = "sync", "tensor", "vector", "scalar", "gpsimd"
ENGS = (SP, PE, VE, SE, PO)


class Ctr:
    def __init__(self, stack, nc, name, dma=False):
        self.h = stack.enter_context(nc.semaphore(name))
        self.n = 0
        self.dma = dma

    def take(self, k=1):
        self.n += k
        return self.n


class Plan:
    def __init__(self, barctr, compctr):
        self.ops = {e: [] for e in ENGS}
        self.barctr = barctr
        self.compctr = compctr

    def do(self, eng, build, inc=None, waits=(), drain=False):
        tok = None
        incs = []
        is_dma = False
        if inc is not None:
            ctr, k = inc if isinstance(inc, tuple) else (inc, 1)
            tok = ctr.take(k)
            incs.append((ctr, k))
            is_dma = ctr.dma
        wl = []
        if is_dma:
            prev = tok - k if inc is not None else None
            if prev:
                wl.append((ctr.h, prev))
        for (c, v) in waits:
            if not v or v <= 0:
                continue
            wl.append((c.h, v))
        self.ops[eng].append({
            "build": build,
            "incs": incs,
            "waits": wl,
            "drain": drain,
            "dma": is_dma,
        })
        return tok

    def play(self, nc, start_snap):
        for e in ENGS:
            lst = self.ops[e]
            if not lst:
                continue
            last = lst[-1]
            if last["dma"]:
                ctr = self.barctr[e] if self.barctr[e].dma else None
                if ctr is None:
                    continue
            else:
                ctr = self.compctr[e]
            if not any(c is ctr for (c, _k) in last["incs"]):
                k = 16 if ctr.dma else 1
                ctr.take(k)
                last["incs"].append((ctr, k))
        with nc.Block() as blk:
            for e in ENGS:
                lst = self.ops[e]
                if not lst:
                    continue

                def run(engine, lst=lst):
                    if start_snap is not None:
                        for (h, v) in start_snap:
                            if v > 0:
                                engine.wait_ge(h, v)
                    for ent in lst:
                        for (h, v) in ent["waits"]:
                            engine.wait_ge(h, v)
                        if ent["drain"]:
                            engine.drain()
                        inst = ent["build"](engine)
                        for (c, k) in ent["incs"]:
                            inst.then_inc(c.h, k)
                getattr(blk, e)(run)


def _to_bf(x):
    return np.ascontiguousarray(np.asarray(x).astype(BF16NP))


class Arena:
    def __init__(self, t, total, name):
        self.t = t
        self.total = total
        self.used = []
        self.hw = 0
        self.name = name

    A = 32

    def alloc(self, name, n):
        nr = -(-n // self.A) * self.A
        self.used.sort()
        pos = 0
        for (s, e, _nm) in self.used:
            if s - pos >= nr:
                break
            pos = -(-max(pos, e) // self.A) * self.A
        assert pos + nr <= self.total, \
            f"{self.name} OOM: {name} needs {nr} at {pos} > {self.total}"
        self.used.append((pos, pos + nr, name))
        self.hw = max(self.hw, pos + nr)
        return Buf(self.t, pos, n, self.total, name)

    def alloc_high(self, name, n):
        nr = -(-n // self.A) * self.A
        self.used.sort()
        pos = (self.total - nr) // self.A * self.A
        for (s, e, _nm) in reversed(self.used):
            if pos >= e:
                break
            pos = min(pos, (s - nr) // self.A * self.A)
        assert pos >= 0, f"{self.name} OOM(high): {name} needs {nr}"
        self.used.append((pos, pos + nr, name))
        self.hw = max(self.hw, pos + nr)
        return Buf(self.t, pos, n, self.total, name)

    def free(self, *bufs):
        for buf in bufs:
            for i, (s, e, nm) in enumerate(self.used):
                if s == buf.off and nm == buf.name:
                    self.used.pop(i)
                    break
            else:
                raise KeyError(f"free unknown {buf.name}")


class Buf:
    def __init__(self, t, off, n, rowlen, name):
        self.t, self.off, self.n, self.rowlen, self.name = t, off, n, rowlen, name

    def __getitem__(self, idx):
        ps, cs = idx
        if isinstance(cs, int):
            cs = slice(cs, cs + 1)
        c0 = cs.start or 0
        c1 = cs.stop if cs.stop is not None else self.n
        assert 0 <= c0 <= c1 <= self.n, f"{self.name}[{c0}:{c1}] of {self.n}"
        return self.t[ps, self.off + c0:self.off + c1]

    def ap(self, *dims, off=0):
        return bass.AP(self.t, self.off + off,
                       [[self.rowlen, 128]] + [list(d) for d in dims])


def _host_constants():
    c = {}
    c["ident_f32"] = np.eye(128, dtype=np.float32)
    c["ident_bf"] = _to_bf(np.eye(128))
    j = np.arange(S, dtype=np.float64) - 1024.0
    j2 = j * j
    j2h = j2.astype(BF16NP).astype(np.float64)
    j2l = j2 - j2h
    jh = j.astype(BF16NP).astype(np.float64)
    jl = j - jh
    rows = np.zeros((32, S), dtype=np.float32)
    rows[0], rows[1], rows[2], rows[3], rows[4] = j2h, j2l, jh, jl, 1.0
    jaux = np.zeros((128, S), dtype=np.float32)
    for slot in range(4):
        jaux[slot * 32:slot * 32 + 32] = rows
    c["jaux"] = _to_bf(jaux)
    sel = np.zeros((128, 16 * 128), dtype=np.float32)
    for s in range(16):
        for base in (0, 32, 64, 96):
            sel[base + s, s * 128:(s + 1) * 128] = 1.0
    c["sel16"] = _to_bf(sel)
    sel8 = np.zeros((8, 8 * 128), dtype=np.float32)
    for e in range(8):
        sel8[e, e * 128:(e + 1) * 128] = 1.0
    c["sel8"] = _to_bf(sel8)
    usq = np.zeros((128, NCH * 16), dtype=np.float32)
    for ch in range(NCH):
        for p in range(128):
            usq[p, ch * 16 + (ch * 128 + p) // DH] = 1.0
    c["usq"] = _to_bf(usq)
    return c


def _host_layout(inp):
    d = {}
    x = np.asarray(inp["L0_attn_a_hidden_states"])
    xt = np.ascontiguousarray(x.transpose(2, 0, 1))
    d["_x_percore"] = [np.ascontiguousarray(xt[:, 2 * i:2 * i + 2, :])
                       for i in range(8)]
    ren = {"wq": "L0_attn_a_wq", "wk": "L0_attn_a_wk", "wv": "L0_attn_a_wv",
           "wo": "L0_attn_a_wo", "wfq": "L0_attn_a_wfq", "wfg": "L0_attn_a_wfg",
           "sw1": "L0_ffn_sw1", "sw2": "L0_ffn_sw2", "sw3": "L0_ffn_sw3",
           "gate_w": "L0_ffn_gate_w", "ew1": "L0_ffn_ew1", "ew2": "L0_ffn_ew2",
           "ew3": "L0_ffn_ew3"}
    for k, v in ren.items():
        d[k] = np.ascontiguousarray(np.asarray(inp[v], dtype=np.float32))
    d["W_inT"] = np.ascontiguousarray(np.asarray(inp["L0_attn_b_W_in"]).T)
    d["W_outT"] = np.ascontiguousarray(np.asarray(inp["L0_attn_b_W_out"]).T)
    wx = np.asarray(inp["L0_attn_b_W_x"])
    wxp = np.zeros((80, DI), dtype=np.float32)
    wxp[0] = wx[0]
    wxp[32:48] = wx[1:17]
    wxp[64:80] = wx[17:33]
    d["W_xTp"] = np.ascontiguousarray(wxp.T)
    cw = np.asarray(inp["L0_attn_b_conv_w"]).reshape(DI, 4)
    d["conv_w"] = np.ascontiguousarray(
        cw.reshape(16, 128, 4).transpose(1, 0, 2).reshape(128, 64))
    d["conv_b"] = np.ascontiguousarray(
        np.asarray(inp["L0_attn_b_conv_b"]).reshape(16, 128).T)
    d["A_log"] = np.ascontiguousarray(
        np.asarray(inp["L0_attn_b_A_log"]).reshape(16, 128, 16)
        .transpose(1, 0, 2).reshape(128, 256))
    for short, nm in (("bq", "L0_attn_a_bq"), ("bk", "L0_attn_a_bk"),
                      ("bv", "L0_attn_a_bv"), ("bo", "L0_attn_a_bo"),
                      ("bfq", "L0_attn_a_bfq"), ("bfg", "L0_attn_a_bfg")):
        d[short] = np.ascontiguousarray(np.asarray(inp[nm]).reshape(8, 128).T)
    up = np.asarray(inp["L0_attn_a_up"]).reshape(H, DH)
    uz = np.asarray(inp["L0_attn_a_uz"]).reshape(H, DH)
    upz = np.zeros((128, NCH * 32), dtype=np.float32)
    for ch in range(NCH):
        for p in range(128):
            dm = ch * 128 + p
            upz[p, ch * 32 + dm // DH] = up[dm // DH, dm % DH]
            upz[p, ch * 32 + 16 + dm // DH] = uz[dm // DH, dm % DH]
    d["U_pz"] = upz
    return d


def build_nc(upto="attn", races=True):
    nc = bass.Bass("TRN2", target_bir_lowering=False,
                   detect_race_conditions=races)
    g = ExitStack()

    xin = nc.dram_tensor("xin", [DM, NB, S], F32, kind="ExternalInput").ap()
    out_d = nc.dram_tensor("xout", [NCH, 128, NB, S], F32,
                           kind="ExternalOutput").ap()
    wdr = {}
    for nm, shp in (
        ("wq", [DM, DM]), ("wk", [DM, DM]), ("wv", [DM, DM]), ("wo", [DM, DM]),
        ("wfq", [DM, DM]), ("wfg", [DM, DM]),
        ("sw1", [DM, HF]), ("sw2", [DM, HF]), ("sw3", [HF, DM]),
        ("gate_w", [DM, NEXP]),
        ("ew1", [NEXP, DM, HF]), ("ew2", [NEXP, DM, HF]), ("ew3", [NEXP, HF, DM]),
        ("W_inT", [DM, 2 * DI]), ("W_outT", [DI, DM]), ("W_xTp", [DI, 80]),
        ("conv_w", [128, 64]), ("conv_b", [128, 16]), ("A_log", [128, 256]),
        ("bq", [128, 8]), ("bk", [128, 8]), ("bv", [128, 8]), ("bo", [128, 8]),
        ("bfq", [128, 8]), ("bfg", [128, 8]), ("U_pz", [128, NCH * 32]),
        ("ident_f32", [128, 128]),
    ):
        wdr[nm] = nc.dram_tensor(nm, shp, F32, kind="ExternalInput").ap()
    for nm, shp in (("ident_bf", [128, 128]), ("jaux", [128, S]),
                    ("sel16", [128, 16 * 128]), ("sel8", [8, 8 * 128]),
                    ("usq", [128, NCH * 16])):
        wdr[nm] = nc.dram_tensor(nm, shp, BF16, kind="ExternalInput").ap()

    xsp = nc.dram_tensor("xspill", [128, NCH * T], BF16, kind="Internal").ap()
    oac = nc.dram_tensor("outacc", [128, NCH * T], F32, kind="Internal").ap()
    v_dram = nc.dram_tensor("vscr", [NB, 128, 16 * 1040], BF16, kind="Internal").ap()
    z_d = nc.dram_tensor("zscr", [NB, 128, 16, S], BF16, kind="Internal").ap()
    xc_d = nc.dram_tensor("xcscr", [NB, 128, 16, S], BF16, kind="Internal").ap()

    sb = {}

    def sbt(name, shape, dt):
        sb[name] = g.enter_context(nc.sbuf_tensor(name, shape, dt))
        return sb[name]

    ones_f = sbt("ones_f", [128, 128], F32)
    ones_b = sbt("ones_b", [128, 128], BF16)
    ident_f = sbt("ident_f", [128, 128], F32)
    ident_b = sbt("ident_b", [128, 128], BF16)
    jaux_s = sbt("jaux_s", [128, S], BF16)
    sel16_s = sbt("sel16_s", [128, 16 * 128], BF16)
    sel8_s = sbt("sel8_s", [8, 8 * 128], BF16)
    usq_s = sbt("usq_s", [128, NCH * 16], BF16)
    upz_s = sbt("upz_s", [128, NCH * 32], BF16)
    bias_s = sbt("bias_s", [128, 48], F32)
    BQ, BK, BV, BO, BFQG, BOBV = 0, 8, 16, 24, 32, 40
    bv16_s = sbt("bv16_s", [128, 8], BF16)
    eps_s = sbt("eps_s", [128, 1], F32)
    acol_s = sbt("acol_s", [128, 256], F32)
    cw_s = sbt("cw_s", [128, 64], F32)
    cb_s = sbt("cb_s", [128, 16], F32)
    wx_s = sbt("wx_s", [128, 16 * 80], BF16)
    gw_s = sbt("gw_s", [128, NCH * 8], BF16)

    ABF = 78400
    AFS = 8500
    abf_t = sbt("abf", [128, ABF], BF16)
    af_t = sbt("af32", [128, AFS], F32)
    abf = Arena(abf_t, ABF, "abf")
    af32 = Arena(af_t, AFS, "af32")

    pb = [g.enter_context(nc.psum_tensor(f"pb{i}", [128, 512], F32))
          for i in range(7)]
    pbf = g.enter_context(nc.psum_tensor("pbf", [128, 1024], BF16))

    cPE = Ctr(g, nc, "cPE"); cVE = Ctr(g, nc, "cVE"); cSE = Ctr(g, nc, "cSE")
    cPO = Ctr(g, nc, "cPO")
    cSD = Ctr(g, nc, "cSD", dma=True); cGD = Ctr(g, nc, "cGD", dma=True)
    cSP = Ctr(g, nc, "cSP")
    ALLC = (cPE, cVE, cSE, cPO, cSD, cGD, cSP)
    BARC = {SP: cSD, PE: cPE, VE: cVE, SE: cSE, PO: cGD}
    COMPC = {SP: cSP, PE: cPE, VE: cVE, SE: cSE, PO: cPO}

    def snap():
        return [(c.h, c.n) for c in ALLC]

    prev = [None]

    def play(P):
        P.play(nc, prev[0])
        prev[0] = snap()

    def newplan():
        return Plan(BARC, COMPC)

    xin_r = xin.rearrange("(c p) b n -> p b c n", p=128)

    P = newplan()
    for nm, dst in (("ident_f32", ident_f), ("ident_bf", ident_b),
                    ("jaux", jaux_s), ("sel16", sel16_s), ("sel8", sel8_s),
                    ("usq", usq_s), ("A_log", acol_s), ("conv_w", cw_s),
                    ("conv_b", cb_s)):
        P.do(SP, lambda e, nm=nm, dst=dst: e.dma_start(out=dst[:], in_=wdr[nm]),
             inc=(cSD, 16))
    for nm, col in (("bq", BQ), ("bk", BK), ("bv", BV), ("bo", BO),
                    ("bfq", BFQG), ("bfg", BFQG + 8)):
        P.do(SP, lambda e, nm=nm, col=col: e.dma_start(
            out=bias_s[:, col:col + 8], in_=wdr[nm]), inc=(cSD, 16))
    sd0 = cSD.n
    P.do(PO, lambda e: e.dma_start(out=upz_s[:], in_=wdr["U_pz"]), inc=(cGD, 16))
    P.do(PO, lambda e: e.dma_start(
        out=bass.AP(wx_s, 0, [[16 * 80, 128], [80, 16], [1, 80]]),
        in_=wdr["W_xTp"].rearrange("(c p) n -> p c n", p=128)), inc=(cGD, 16))
    P.do(PO, lambda e: e.dma_start(
        out=bass.AP(gw_s, 0, [[NCH * 8, 128], [8, NCH], [1, 8]]),
        in_=wdr["gate_w"].rearrange("(c p) n -> p c n", p=128)), inc=(cGD, 16))
    P.do(VE, lambda e: e.memset(ones_f[:], 1.0))
    P.do(VE, lambda e: e.memset(eps_s[:], 1e-5))
    P.do(VE, lambda e: e.memset(ones_b[:], 1.0))
    P.do(VE, lambda e: e.tensor_scalar_mul(
        bias_s[:, BQ:BQ + 8], bias_s[:, BQ:BQ + 8], 0.125), waits=[(cSD, sd0)])
    P.do(VE, lambda e: e.tensor_add(
        bias_s[:, BFQG:BFQG + 8], bias_s[:, BFQG:BFQG + 8],
        bias_s[:, BFQG + 8:BFQG + 16]), drain=True)
    P.do(VE, lambda e: e.tensor_copy(bv16_s[:], bias_s[:, BV:BV + 8]), drain=True)
    se_a = P.do(SE, lambda e: e.activation(acol_s[:], acol_s[:], AF.Exp),
                inc=cSE, waits=[(cSD, sd0)])
    P.do(VE, lambda e: e.tensor_scalar_mul(acol_s[:], acol_s[:], -1.0),
         inc=cVE, waits=[(cSE, se_a)], drain=True)
    play(P)

    def phase_attn(b):
        if upto == "p0":
            return
        P = newplan()
        xh = abf.alloc("xh", NCH * S)
        wsl = [abf.alloc(f"wsl{i}", NCH * 1024) for i in range(2)]
        mq = abf.alloc("mq", NCH * S)
        mk = abf.alloc("mk", NCH * S)
        sq8 = abf.alloc("sq8", NCH * TC)
        xc16 = abf.alloc("xc16", NCH * TC)
        nmrep = abf.alloc("nmrep", TC)
        rsrep = abf.alloc("rsrep", TC)
        ttmp = abf.alloc("ttmp", TC)
        glo16 = abf.alloc("glo16", 8)
        bst16 = abf.alloc("bst16", 1)
        bstrow = abf.alloc("bstrow", 16)
        vroll = [abf.alloc(f"vroll{i}", 1040) for i in range(2)]

        stg = af32.alloc("stg", NCH * TC)
        rows = af32.alloc("rows", 4 * TC)
        qb = af32.alloc("qb", 6)
        glo = af32.alloc("glo", 8)
        gvec = af32.alloc("gvec", 8)
        bstrep = af32.alloc("bstrep", 16)
        cmp_ = af32.alloc("cmp", 144 + 512)
        O_E, O_D, O_RP, O_U, O_RZ, O_TA, O_T2, O_CF = \
            0, 32, 64, 80, 96, 112, 128, 144
        rq = af32.alloc("rq", 4)

        def xhc(c, t0, t1):
            return xh[:, c * S + t0:c * S + t1]

        po_cast, se_sq, pe_ln, ve_var, se_rstd, pe_rep, ve_xln, sd_x = \
            [], [], [], [], [], [], [], []
        for tc in range(NTC_B):
            w = [(cPO, po_cast[tc - 1])] if tc >= 1 else []
            sd_x.append(P.do(SP, lambda e, tc=tc: e.dma_start(
                out=stg.ap([TC, NCH], [1, TC]),
                in_=xin_r[:, b, :, tc * TC:(tc + 1) * TC]),
                inc=(cSD, 16), waits=w))
            w = [(cSD, sd_x[tc])]
            if tc >= 1:
                w += [(cVE, ve_xln[tc - 1]), (cSE, se_sq[tc - 1]),
                      (cPE, pe_ln[tc - 1])]
            po_cast.append(P.do(PO, lambda e: e.tensor_copy(xc16[:, :], stg[:, :]),
                                inc=cPO, waits=w))
            w = [(cPO, po_cast[tc])]
            if tc >= 1:
                w.append((cPE, pe_ln[tc - 1]))
            tok = None
            for c in range(NCH):
                tok = P.do(SE, lambda e, c=c: e.activation(
                    sq8[:, c * TC:(c + 1) * TC], xc16[:, c * TC:(c + 1) * TC],
                    AF.Square), inc=cSE if c == NCH - 1 else None,
                    waits=w if c == 0 else ())
            se_sq.append(tok)
            w = [(cPO, po_cast[tc])]
            if tc >= 1:
                w.append((cVE, ve_var[tc - 1]))
            for c in range(NCH):
                P.do(PE, lambda e, c=c: e.matmul(
                    pb[0][0:1, :], ones_b[:, 0:1], xc16[:, c * TC:(c + 1) * TC],
                    start=(c == 0), stop=(c == NCH - 1)), waits=w if c == 0 else ())
            for c in range(NCH):
                tok = P.do(PE, lambda e, c=c: e.matmul(
                    pb[1][0:1, :], ones_b[:, 0:1], sq8[:, c * TC:(c + 1) * TC],
                    start=(c == 0), stop=(c == NCH - 1)),
                    inc=cPE if c == NCH - 1 else None,
                    waits=[(cSE, se_sq[tc])] if c == 0 else ())
            pe_ln.append(tok)
            w = [(cPE, pe_ln[tc])]
            if tc >= 1:
                w.append((cPE, pe_rep[tc - 1]))
            P.do(VE, lambda e: e.tensor_scalar_mul(
                rows[0:1, 0:TC], pb[0][0:1, :], -1.0 / DM), waits=w)
            P.do(VE, lambda e: e.tensor_scalar_mul(
                rows[0:1, TC:2 * TC], pb[1][0:1, :], 1.0 / DM))
            P.do(VE, lambda e: e.tensor_mul(
                rows[0:1, 2 * TC:3 * TC], rows[0:1, 0:TC], rows[0:1, 0:TC]),
                drain=True)
            ve_var.append(P.do(VE, lambda e: e.tensor_sub(
                rows[0:1, 2 * TC:3 * TC], rows[0:1, TC:2 * TC],
                rows[0:1, 2 * TC:3 * TC]), inc=cVE, drain=True))
            P.do(SE, lambda e: e.activation(
                rows[0:1, 2 * TC:3 * TC], rows[0:1, 2 * TC:3 * TC], AF.Ln,
                bias=eps_s[0:1, 0:1]), waits=[(cVE, ve_var[tc])])
            se_rstd.append(P.do(SE, lambda e: e.activation(
                rows[0:1, 3 * TC:4 * TC], rows[0:1, 2 * TC:3 * TC], AF.Exp,
                scale=-0.5), inc=cSE, drain=True))
            wrep = [(cSE, se_rstd[tc])]
            if tc >= 1:
                wrep.append((cVE, ve_xln[tc - 1]))
            P.do(PE, lambda e: e.matmul(
                pb[2][:], ones_f[0:1, :], rows[0:1, 0:TC], start=True, stop=True),
                waits=wrep)
            pe_rep.append(P.do(PE, lambda e: e.matmul(
                pb[3][:], ones_f[0:1, :], rows[0:1, 3 * TC:4 * TC],
                start=True, stop=True), inc=cPE))
            P.do(VE, lambda e: e.tensor_copy(nmrep[:, :], pb[2][:]),
                 waits=[(cPE, pe_rep[tc])])
            P.do(VE, lambda e: e.tensor_copy(rsrep[:, :], pb[3][:]))
            for c in range(NCH):
                P.do(VE, lambda e, c=c: e.tensor_add(
                    ttmp[:, :], xc16[:, c * TC:(c + 1) * TC], nmrep[:, :]),
                    drain=True)
                tok = P.do(VE, lambda e, c=c, tc=tc: e.tensor_mul(
                    xhc(c, tc * TC, (tc + 1) * TC), ttmp[:, :], rsrep[:, :]),
                    inc=cVE if c == NCH - 1 else None, drain=True)
            ve_xln.append(tok)

        if upto == "pa1":
            play(P)
            abf.used = [u for u in abf.used]
            abf.used.clear()
            af32.used.clear()
            return
        wnames = ["wq", "wk", "wv", "wfg", "wfq", "wo"]
        gd_w = {}
        ve_wsc = {}

        def load_w(wn, wait_pe=None, wait_ve=None):
            slot = wsl[wnames.index(wn) % 2]
            w = []
            if wait_pe:
                w.append((cPE, wait_pe))
            if wait_ve:
                w.append((cVE, wait_ve))
            gd_w[wn] = P.do(PO, lambda e, wn=wn, slot=slot: e.dma_start(
                out=slot.ap([1024, NCH], [1, 1024]),
                in_=wdr[wn].rearrange("(c p) n -> p c n", p=128)),
                inc=(cGD, 16), waits=w)

        def scale_w(wn, sc):
            slot = wsl[wnames.index(wn) % 2]
            ve_wsc[wn] = P.do(VE, lambda e, slot=slot, sc=sc: e.tensor_scalar_mul(
                slot[:, :], slot[:, :], sc), inc=cVE, waits=[(cGD, gd_w[wn])])

        def proj_pass(wn, dst, bias_col, bnd_col, prev_bnd=None):
            slot = wsl[wnames.index(wn) % 2]
            se_ev = []
            pe_g = []
            pe_usq = []
            ve_bnd = []
            for tc in range(NTC_B):
                sq_toks = []
                for fc in range(NCH):
                    gi = tc * NCH + fc
                    bank = pb[gi % 2]
                    w = []
                    if gi == 0:
                        w = [(cGD, gd_w[wn]), (cVE, ve_xln[NTC_B - 1])]
                        if wn in ve_wsc:
                            w.append((cVE, ve_wsc[wn]))
                    if gi >= 2:
                        w.append((cSE, se_ev[gi - 2]))
                    for kc in range(NCH):
                        tok = P.do(PE, lambda e, kc=kc, fc=fc, tc=tc, bank=bank,
                                   slot=slot: e.matmul(
                            bank[:],
                            slot[:, kc * 1024 + fc * 128:kc * 1024 + fc * 128 + 128],
                            xhc(kc, tc * TC, (tc + 1) * TC),
                            start=(kc == 0), stop=(kc == NCH - 1)),
                            inc=cPE if kc == NCH - 1 else None,
                            waits=w if kc == 0 else ())
                    pe_g.append(tok)
                    w2 = [(cPE, pe_g[-1])]
                    if tc >= 1 and fc == 0:
                        w2.append((cPE, pe_usq[tc - 1]))
                    if tc == 0 and fc == 0:
                        w2.append((cPE, pe_ln[NTC_B - 1]))
                    se_ev.append(P.do(SE, lambda e, bank=bank, fc=fc, tc=tc:
                                      e.activation(
                        dst[:, fc * S + tc * TC:fc * S + (tc + 1) * TC], bank[:],
                        AF.Identity, bias=bias_s[:, bias_col + fc:bias_col + fc + 1]),
                        inc=cSE, waits=w2))
                    sq_toks.append(P.do(SE, lambda e, fc=fc, tc=tc: e.activation(
                        sq8[:, fc * TC:(fc + 1) * TC],
                        dst[:, fc * S + tc * TC:fc * S + (tc + 1) * TC],
                        AF.Square), inc=cSE, drain=True))
                for fc in range(NCH):
                    w3 = [(cSE, sq_toks[fc])]
                    if fc == 0 and tc >= 1:
                        w3.append((cVE, ve_bnd[tc - 1]))
                    if fc == 0 and tc == 0 and wn == "wq":
                        w3.append((cVE, ve_xln[NTC_B - 1]))
                    if fc == 0 and tc == 0 and prev_bnd:
                        w3.append((cVE, prev_bnd))
                    tok = P.do(PE, lambda e, fc=fc: e.matmul(
                        pb[2][0:16, :], usq_s[:, fc * 16:(fc + 1) * 16],
                        sq8[:, fc * TC:(fc + 1) * TC],
                        start=(fc == 0), stop=(fc == NCH - 1)),
                        inc=cPE if fc == NCH - 1 else None, waits=w3)
                pe_usq.append(tok)
                P.do(VE, lambda e: e.tensor_reduce(
                    rq[0:16, 0:1], pb[2][0:16, :], AX.X, ALU.max),
                    waits=[(cPE, pe_usq[tc])])
                if tc == 0:
                    t = P.do(VE, lambda e: e.tensor_copy(
                        qb[0:16, bnd_col], rq[0:16, 0:1]), inc=cVE, drain=True)
                else:
                    t = P.do(VE, lambda e: e.tensor_max(
                        qb[0:16, bnd_col], qb[0:16, bnd_col], rq[0:16, 0:1]),
                        inc=cVE, drain=True)
                ve_bnd.append(t)
            return pe_g[-1], ve_bnd[-1]

        load_w("wq")
        scale_w("wq", 0.125)
        load_w("wk")
        pe_q_done, ve_qb = proj_pass("wq", mq, BQ, 0)
        load_w("wv", wait_pe=pe_q_done)
        pe_k_done, ve_kb = proj_pass("wk", mk, BK, 1, prev_bnd=ve_qb)
        load_w("wfg", wait_pe=pe_k_done)
        scale_w("wfg", 8.0)

        if upto == "proj":
            play(P)
            abf.used = [u for u in abf.used]
            abf.used.clear()
            af32.used.clear()
            return
        slotv = wsl[wnames.index("wv") % 2]
        sd_v = []
        ve_vev = []
        gi = 0
        for tc in range(NTC_B):
            for tt in range(4):
                tg = tc * 4 + tt
                vr = vroll[tg % 2]
                if tg < 2:
                    P.do(VE, lambda e, vr=vr: e.memset(
                        vr.ap([65, 16], off=64), 1.0))
                for fcol in range(2):
                    bank = pb[gi % 2]
                    w = []
                    if gi == 0:
                        w = [(cGD, gd_w["wv"]), (cSE, cSE.n)]
                    if gi >= 2:
                        w.append((cVE, ve_vev[gi - 2]))
                    for kc in range(NCH):
                        tok = P.do(PE, lambda e, kc=kc, tc=tc, tt=tt, fcol=fcol,
                                   bank=bank: e.matmul(
                            bank[:],
                            xhc(kc, tc * TC + tt * 128, tc * TC + tt * 128 + 128),
                            slotv[:, kc * 1024 + fcol * 512:kc * 1024 + fcol * 512 + 512],
                            start=(kc == 0), stop=(kc == NCH - 1)),
                            inc=cPE if kc == NCH - 1 else None,
                            waits=w if kc == 0 else ())
                    w2 = [(cPE, tok)]
                    if tg >= 2 and fcol == 0:
                        w2.append((cSD, sd_v[tg - 2]))
                    ve_vev.append(P.do(VE, lambda e, bank=bank, vr=vr, fcol=fcol:
                                       e.tensor_copy(
                        vr.ap([65, 8], [1, 64], off=fcol * 8 * 65), bank[:]),
                        inc=cVE, waits=w2))
                    gi += 1
                sd_v.append(P.do(SP, lambda e, vr=vr, tg=tg: e.dma_start(
                    out=v_dram[b, :, tg * 1040:(tg + 1) * 1040], in_=vr[:, :]),
                    inc=(cSD, 16), waits=[(cVE, ve_vev[-1])]))
        pe_v_done = tok

        load_w("wfq", wait_pe=pe_v_done)
        scale_w("wfq", 8.0)

        if upto == "wfq":
            play(P)
            abf.used.clear()
            af32.used.clear()
            return
        for c in range(NCH):
            P.do(VE, lambda e, c=c: e.tensor_reduce(
                glo[:, c], mq[:, c * S:(c + 1) * S], AX.X, ALU.add),
                waits=[(cSE, cSE.n)] if c == 0 else ())
        P.do(VE, lambda e: e.tensor_scalar_mul(glo[:, 0:8], glo[:, 0:8], 1.0 / S),
             drain=True)
        ve_glo = P.do(VE, lambda e: e.tensor_copy(glo16[:, :], glo[:, 0:8]),
                      inc=cVE, drain=True)
        slotfg = wsl[wnames.index("wfg") % 2]
        first = [True]
        for fc in range(NCH):
            for kc in range(NCH):
                w = []
                if first[0]:
                    w = [(cVE, ve_glo), (cVE, ve_wsc["wfg"])]
                st_ = first[0]
                tok = P.do(PE, lambda e, fc=fc, kc=kc, st_=st_: e.matmul(
                    pb[3][:, fc:fc + 1],
                    slotfg[:, kc * 1024 + fc * 128:kc * 1024 + fc * 128 + 128],
                    glo16[:, kc], start=st_,
                    stop=(fc == NCH - 1 and kc == NCH - 1)),
                    inc=cPE if (fc == NCH - 1 and kc == NCH - 1) else None,
                    waits=w)
                first[0] = False
        pe_gv = tok
        ve_gv = P.do(VE, lambda e: e.tensor_add(
            gvec[:, 0:8], pb[3][:, 0:8], bias_s[:, BFQG:BFQG + 8]),
            inc=cVE, waits=[(cPE, pe_gv)])

        if upto == "gvec":
            play(P)
            abf.used.clear()
            af32.used.clear()
            return
        P.do(SE, lambda e: e.activation(qb[0:16, 2], qb[0:16, 0], AF.Ln),
             waits=[(cVE, ve_qb), (cVE, ve_kb)])
        P.do(SE, lambda e: e.activation(qb[0:16, 2], qb[0:16, 2], AF.Exp,
                                        scale=0.5), drain=True)
        P.do(SE, lambda e: e.activation(qb[0:16, 3], qb[0:16, 1], AF.Ln))
        se_bn = P.do(SE, lambda e: e.activation(qb[0:16, 3], qb[0:16, 3], AF.Exp,
                                                scale=0.5), inc=cSE, drain=True)
        P.do(VE, lambda e: e.tensor_mul(qb[0:16, 4], qb[0:16, 2], qb[0:16, 3]),
             waits=[(cSE, se_bn)], drain=True)
        ve_bst = P.do(VE, lambda e: e.tensor_copy(bst16[0:16, 0:1], qb[0:16, 4]),
                      inc=cVE, drain=True)
        pe_bt = P.do(PE, lambda e: e.transpose(
            pbf[0:1, 0:16], bst16[0:16, 0:1], ident_b[0:16, 0:16]),
            inc=cPE, waits=[(cVE, ve_bst)])
        ve_btr = P.do(VE, lambda e: e.tensor_copy(
            bstrow[0:1, 0:16], pbf[0:1, 0:16]), inc=cVE, waits=[(cPE, pe_bt)])
        pe_br = P.do(PE, lambda e: e.matmul(
            pb[4][:, 0:16], ones_b[0:1, :], bstrow[0:1, 0:16],
            start=True, stop=True), inc=cPE, waits=[(cVE, ve_btr)])
        ve_brep = P.do(VE, lambda e: e.tensor_copy(
            bstrep[:, 0:16], pb[4][:, 0:16]), inc=cVE, waits=[(cPE, pe_br)])

        if upto == "wv":
            play(P)
            abf.used = [u for u in abf.used]
            abf.used.clear()
            af32.used.clear()
            return
        if upto == "bst":
            play(P)
            abf.used.clear()
            af32.used.clear()
            return
        abf.free(xh, sq8, xc16, nmrep, rsrep, ttmp)
        c16 = abf.alloc_high("c16", NCH * TC)
        abc = abf.alloc_high("abc", 4 * S)
        expb = [abf.alloc_high(f"expb{i}", TC) for i in range(4)]
        ctxp = [abf.alloc_high(f"ctxp{i}", 128) for i in range(4)]
        ctxall = abf.alloc_high("ctxall", NCH * TC)
        otile = abf.alloc_high("otile", NCH * TC)

        P.do(VE, lambda e: e.memset(cmp_[:, O_CF:O_CF + 512], 0.0))
        slotfq = wsl[wnames.index("wfq") % 2]
        se_c = []
        pe_pz = []
        ve_abc = []
        pe_tr = []
        for tc in range(NTC_B):
            for fc in range(NCH):
                gi2 = tc * NCH + fc
                bank = pb[gi2 % 2]
                w = []
                if gi2 == 0:
                    w = [(cVE, ve_gv), (cVE, ve_wsc["wfq"])]
                if gi2 >= 2:
                    w.append((cSE, se_c[gi2 - 2]))
                for kc in range(NCH):
                    tok = P.do(PE, lambda e, kc=kc, fc=fc, tc=tc, bank=bank:
                               e.matmul(
                        bank[:],
                        slotfq[:, kc * 1024 + fc * 128:kc * 1024 + fc * 128 + 128],
                        mq[:, kc * S + tc * TC:kc * S + (tc + 1) * TC],
                        start=(kc == 0), stop=(kc == NCH - 1)),
                        inc=cPE if kc == NCH - 1 else None,
                        waits=w if kc == 0 else ())
                w2 = [(cPE, tok)]
                if tc >= 1 and fc < 2 and pe_pz:
                    w2.append((cPE, pe_pz[4 * (tc - 1) + 3]))
                se_c.append(P.do(SE, lambda e, bank=bank, fc=fc: e.activation(
                    c16[:, fc * TC:(fc + 1) * TC], bank[:], AF.Tanh,
                    bias=gvec[:, fc]), inc=cSE, waits=w2))
            for tt in range(4 if upto not in ("cmm",) else 0):
                w = [(cSE, se_c[tc * NCH + NCH - 1])]
                if ve_abc:
                    w.append((cVE, ve_abc[-1]))
                else:
                    w.append((cVE, ve_brep))
                for fc in range(NCH):
                    tok = P.do(PE, lambda e, fc=fc, tc=tc, tt=tt: e.matmul(
                        pb[4][:, 0:32],
                        c16[:, fc * TC + tt * 128:fc * TC + tt * 128 + 128],
                        upz_s[:, fc * 32:(fc + 1) * 32],
                        start=(fc == 0), stop=(fc == NCH - 1)),
                        inc=cPE if fc == NCH - 1 else None,
                        waits=w if fc == 0 else ())
                pe_pz.append(tok)
                se_e = P.do(SE, lambda e: e.activation(
                    cmp_[:, O_E:O_E + 32], pb[4][:, 0:32], AF.Exp, scale=-1.0),
                    inc=cSE, waits=[(cPE, pe_pz[-1])])
                P.do(VE, lambda e: e.tensor_scalar_add(
                    cmp_[:, O_D:O_D + 32], cmp_[:, O_E:O_E + 32], 1.0),
                    waits=[(cSE, se_e)])
                P.do(VE, lambda e: e.reciprocal(
                    cmp_[:, O_RP:O_RP + 16], cmp_[:, O_D:O_D + 16]), drain=True)
                P.do(VE, lambda e: e.tensor_scalar(
                    cmp_[:, O_U:O_U + 16], cmp_[:, O_RP:O_RP + 16],
                    2048.0, -1024.0, ALU.mult, ALU.add), drain=True)
                P.do(VE, lambda e: e.tensor_scalar_mul(
                    cmp_[:, O_RZ:O_RZ + 16], cmp_[:, O_D + 16:O_D + 32],
                    1.0 / 2048.0), drain=True)
                P.do(VE, lambda e: e.tensor_mul(
                    cmp_[:, O_TA:O_TA + 16], cmp_[:, O_RZ:O_RZ + 16],
                    cmp_[:, O_RZ:O_RZ + 16]), drain=True)

                if upto == "chain1":
                    continue
                def dst_ap(compoff):
                    return bass.AP(af_t, cmp_.off + O_CF + compoff,
                                   [[AFS, 128], [128, 4], [32, 4]])
                P.do(VE, lambda e: e.tensor_scalar_mul(
                    dst_ap(0), cmp_[:, O_TA:O_TA + 16], -2.0), drain=True)
                P.do(VE, lambda e: e.tensor_scalar_mul(
                    dst_ap(1), cmp_[:, O_TA:O_TA + 16], -2.0))
                P.do(VE, lambda e: e.tensor_scalar_mul(
                    cmp_[:, O_TA:O_TA + 16], cmp_[:, O_TA:O_TA + 16], -2.0),
                    drain=True)
                P.do(VE, lambda e: e.tensor_mul(
                    cmp_[:, O_T2:O_T2 + 16], cmp_[:, O_TA:O_TA + 16],
                    cmp_[:, O_U:O_U + 16]), drain=True)
                P.do(VE, lambda e: e.tensor_scalar_mul(
                    dst_ap(2), cmp_[:, O_T2:O_T2 + 16], -2.0), drain=True)
                P.do(VE, lambda e: e.tensor_scalar_mul(
                    dst_ap(3), cmp_[:, O_T2:O_T2 + 16], -2.0))
                P.do(VE, lambda e: e.tensor_mul(
                    cmp_[:, O_T2:O_T2 + 16], cmp_[:, O_T2:O_T2 + 16],
                    cmp_[:, O_U:O_U + 16]), drain=True)
                ve_cf = P.do(VE, lambda e: e.tensor_sub(
                    dst_ap(4), cmp_[:, O_T2:O_T2 + 16], bstrep[:, 0:16]),
                    inc=cVE, drain=True)
                for gg in range(4 if upto != "notr" else 0):
                    gti = len(pe_tr)
                    bank = pb[5 + gg % 2]
                    w = [(cVE, ve_cf)] if gg == 0 else []
                    if gti >= 2:
                        w.append((cVE, ve_abc[gti - 2]))
                    pe_tr.append(P.do(PE, lambda e, gg=gg, bank=bank: e.transpose(
                        bank[:, 0:128],
                        cmp_[:, O_CF + gg * 128:O_CF + (gg + 1) * 128],
                        ident_f[:]), inc=cPE, waits=w))
                    ve_abc.append(P.do(VE, lambda e, gg=gg, tc=tc, tt=tt,
                                       bank=bank: e.tensor_copy(
                        abc[:, gg * S + tc * TC + tt * 128:
                            gg * S + tc * TC + tt * 128 + 128],
                        bank[:, 0:128]), inc=cVE, waits=[(cPE, pe_tr[-1])]))

        if upto in ("cmm", "cfull", "notr", "pzonly", "chain1"):
            play(P)
            abf.used.clear()
            af32.used.clear()
            return
        load_w("wo", wait_pe=pe_gv)
        slotwo = wsl[wnames.index("wo") % 2]
        first = [True]
        for fc in range(NCH):
            for kc in range(NCH):
                w = []
                if first[0]:
                    w = [(cGD, gd_w["wo"]), (cVE, ve_brep)]
                    if ve_abc:
                        w.append((cVE, ve_abc[-1]))
                st_ = first[0]
                tok = P.do(PE, lambda e, fc=fc, kc=kc, st_=st_: e.matmul(
                    pb[6][:, fc:fc + 1],
                    slotwo[:, kc * 1024 + fc * 128:kc * 1024 + fc * 128 + 128],
                    bv16_s[:, kc:kc + 1], start=st_,
                    stop=(fc == NCH - 1 and kc == NCH - 1)),
                    inc=cPE if (fc == NCH - 1 and kc == NCH - 1) else None,
                    waits=w)
                first[0] = False
        ve_bobv = P.do(VE, lambda e: e.tensor_add(
            bias_s[:, BOBV:BOBV + 8], pb[6][:, 0:8], bias_s[:, BO:BO + 8]),
            inc=cVE, waits=[(cPE, tok)])

        if upto == "cpass":
            play(P)
            abf.used = [u for u in abf.used]
            abf.used.clear()
            af32.used.clear()
            return
        abf.free(wsl[0], c16, vroll[0], vroll[1], glo16, bst16, bstrow)
        vpb = [abf.alloc(f"vpb{i}", 1040) for i in range(16)]
        sd_vr = None
        for i in range(16):
            w0 = []
            if i == 0:
                w0 = [(cSD, sd_v[-1]), (cPE, pe_pz[-1]), (cVE, ve_abc[-1])]
            sd_vr = P.do(SP, lambda e, i=i: e.dma_start(
                out=vpb[i][:, :], in_=v_dram[b, :, i * 1040:(i + 1) * 1040]),
                inc=(cSD, 16), waits=w0)

        ve_rec = []
        sd_rec = []
        sd_osp = []
        se_exp = {}
        pe_pv = {}
        ve_ctx = {}
        pe_wo_last = [None]
        for qc in range(NTC_B):
            w = [(cPO, po_cast[-1])]
            if qc >= 1:
                w.append((cVE, ve_rec[-1]))
            sd_rec.append(P.do(SP, lambda e, qc=qc: e.dma_start(
                out=stg.ap([TC, NCH], [1, TC]),
                in_=xin_r[:, b, :, qc * TC:(qc + 1) * TC]),
                inc=(cSD, 16), waits=w))
            for h in range(H):
                hc, hr = h // 2, (h % 2) * 64
                slot4 = h % 4
                fg = h // 4
                cbank = pb[2 + (h % 2)]
                for kt in range(16):
                    sbank = pb[kt % 2]
                    w = []
                    if h == 0 and kt == 0:
                        if qc == 0:
                            w += [(cVE, ve_bobv), (cVE, ve_abc[-1]),
                                  (cSD, sd_vr)]
                        else:
                            w.append((cVE, ve_rec[-1]))
                    if kt >= 2:
                        pk = (qc, h, kt - 2)
                    elif h >= 1:
                        pk = (qc, h - 1, 14 + kt)
                    else:
                        pk = (qc - 1, H - 1, 14 + kt)
                    if pk in se_exp:
                        w.append((cSE, se_exp[pk]))
                    P.do(PE, lambda e, hc=hc, hr=hr, kt=kt, qc=qc, sbank=sbank:
                         e.matmul(
                        sbank[:],
                        mk[hr:hr + 64, hc * S + kt * 128:hc * S + kt * 128 + 128],
                        mq[hr:hr + 64, hc * S + qc * TC:hc * S + (qc + 1) * TC],
                        start=True, stop=False), waits=w)
                    tok = P.do(PE, lambda e, slot4=slot4, fg=fg, kt=kt, qc=qc,
                               sbank=sbank: e.matmul(
                        sbank[:],
                        jaux_s[slot4 * 32:slot4 * 32 + 32, kt * 128:kt * 128 + 128],
                        abc[slot4 * 32:slot4 * 32 + 32,
                            fg * S + qc * TC:fg * S + (qc + 1) * TC],
                        start=False, stop=True,
                        tile_position=(slot4 * 32, 0)), inc=cPE)
                    w2 = [(cPE, tok)]
                    if kt >= 4:
                        pv = (qc, h, kt - 4)
                    elif h >= 1:
                        pv = (qc, h - 1, 12 + kt)
                    else:
                        pv = (qc - 1, H - 1, 12 + kt)
                    if pv in pe_pv:
                        w2.append((cPE, pe_pv[pv]))
                    se_exp[(qc, h, kt)] = P.do(
                        SE, lambda e, kt=kt, sbank=sbank: e.activation(
                            expb[kt % 4][:, :], sbank[:], AF.Exp),
                        inc=cSE, waits=w2)
                    w3 = [(cSE, se_exp[(qc, h, kt)])]
                    if kt == 0:
                        ck = (qc, h - 2) if h >= 2 else (qc - 1, H - 2 + h)
                        if ck in ve_ctx:
                            w3.append((cVE, ve_ctx[ck]))
                    for qt in range(4):
                        tok = P.do(PE, lambda e, kt=kt, qt=qt, h=h, cbank=cbank:
                                   e.matmul(
                            cbank[:, qt * 65:qt * 65 + 65],
                            expb[kt % 4][:, qt * 128:qt * 128 + 128],
                            vpb[kt][:, h * 65:(h + 1) * 65],
                            start=(kt == 0 and qt == 0),
                            stop=(kt == 15 and qt == 3)),
                            inc=cPE if qt == 3 else None,
                            waits=w3 if qt == 0 else ())
                    pe_pv[(qc, h, kt)] = tok
                ve_r = P.do(VE, lambda e, cbank=cbank: e.reciprocal(
                    rq[:, 0:4], bass.AP(cbank, 64, [[512, 128], [65, 4]])),
                    inc=cVE, waits=[(cPE, pe_pv[(qc, h, 15)])], drain=True)
                for qt in range(4):
                    w4 = []
                    if qt == 0:
                        w4.append((cVE, ve_r))
                        if h % 2 == 0:
                            ck = (qc, h - 1) if h >= 1 else (qc - 1, H - 1)
                            if ck in ve_ctx:
                                w4.append((cVE, ve_ctx[ck]))
                    tok = P.do(VE, lambda e, qt=qt, h=h, cbank=cbank:
                               e.tensor_single_scalar(
                        ctxp[qt][:, (h % 2) * 64:(h % 2) * 64 + 64],
                        bass.AP(cbank, qt * 65, [[512, 128], [1, 64]]),
                        rq[:, qt], ALU.mult),
                        inc=cVE if qt == 3 else None,
                        waits=w4, drain=(qt == 0))
                ve_ctx[(qc, h)] = tok
                if h % 2 == 1:
                    evtok = None
                    for qt in range(4):
                        w4 = [(cVE, ve_ctx[(qc, h)])] if qt == 0 else []
                        if qt > 0:
                            w4.append((cVE, evtok))
                        if qt == 0 and h == 1 and pe_wo_last[0] is not None:
                            w4.append((cPE, pe_wo_last[0]))
                        ptok = P.do(PE, lambda e, qt=qt: e.transpose(
                            pbf[:, qt * 128:(qt + 1) * 128],
                            ctxp[qt][:, 0:128], ident_b[:]),
                            inc=cPE, waits=w4)
                        evtok = P.do(VE, lambda e, qt=qt, hc=hc: e.tensor_copy(
                            ctxall[:, hc * TC + qt * 128:hc * TC + (qt + 1) * 128],
                            pbf[:, qt * 128:(qt + 1) * 128]),
                            inc=cVE, waits=[(cPE, ptok)])
                    ve_ctx[(qc, h)] = evtok
            pe_wo = []
            for fc in range(NCH):
                bank = pb[4 + fc % 2]
                w = [(cVE, ve_ctx[(qc, H - 1)]), (cSD, sd_rec[qc])]
                if fc == 0 and qc == 0:
                    w.append((cSE, se_c[-1]))
                if fc >= 2:
                    w.append((cVE, ve_rec[-2]))
                if fc < 2 and qc >= 1:
                    w.append((cVE, ve_rec[-2]))
                for kc in range(NCH):
                    tok = P.do(PE, lambda e, kc=kc, fc=fc, bank=bank: e.matmul(
                        bank[:],
                        slotwo[:, kc * 1024 + fc * 128:kc * 1024 + fc * 128 + 128],
                        ctxall[:, kc * TC:(kc + 1) * TC],
                        start=(kc == 0), stop=(kc == NCH - 1)),
                        inc=cPE if kc == NCH - 1 else None,
                        waits=w if kc == 0 else ())
                pe_wo.append(tok)
                wv5 = [(cPE, pe_wo[fc])]
                if qc >= 1 and fc == 0:
                    wv5.append((cSD, sd_osp[-1]))
                ve_rec.append(P.do(VE, lambda e, fc=fc, bank=bank:
                                   e.scalar_tensor_tensor(
                    otile[:, fc * TC:(fc + 1) * TC], bank[:],
                    bias_s[:, BOBV + fc:BOBV + fc + 1], stg[:, fc * TC:(fc + 1) * TC],
                    ALU.add, ALU.add), inc=cVE, waits=wv5))
            pe_wo_last[0] = pe_wo[-1]
            sd_osp.append(P.do(SP, lambda e, qc=qc: e.dma_start(
                out=bass.AP(xsp.tensor, b * S + qc * TC,
                            [[NCH * T, 128], [T, NCH], [1, TC]]),
                in_=otile.ap([TC, NCH], [1, TC])),
                inc=(cSD, 16), waits=[(cVE, ve_rec[-1])]))

        play(P)
        abf.free(mq, mk, wsl[1], abc, expb[0], expb[1], expb[2],
                 expb[3], ctxp[0], ctxp[1], ctxp[2], ctxp[3], ctxall, otile,
                 *vpb)
        af32.free(stg, rows, qb, glo, gvec, bstrep, cmp_, rq)

    phase_attn(0)
    if upto not in ("pa1", "proj", "wv", "wfq", "gvec", "bst", "cmm", "cfull",
                    "notr", "pzonly", "chain1", "p0"):
        phase_attn(1)

    def xsp_ap(tcg, w=TC):
        bq_, tq = divmod(tcg, NTC_B)
        return bass.AP(xsp.tensor, bq_ * S + tq * TC,
                       [[NCH * T, 128], [T, NCH], [1, w]])

    def oac_ap(tcg, w=TC):
        bq_, tq = divmod(tcg, NTC_B)
        return bass.AP(oac.tensor, bq_ * S + tq * TC,
                       [[NCH * T, 128], [T, NCH], [1, w]])

    def phase_ffn(inv):
        P = newplan()
        xw = abf.alloc("xw", NCH * T)
        xt16 = [abf.alloc("xt16A", NCH * TC)]
        xt16.append(xt16[0])
        sq = abf.alloc("sqF", NCH * TC)
        nmrep = abf.alloc("nmrepF", TC)
        rsrep = abf.alloc("rsrepF", TC)
        ttmp = abf.alloc("ttmpF", TC)
        wg = [[abf.alloc(f"w{gi}_{j}", 8 * 512 if j < 2 else 4 * 1024)
               for j in range(3)] for gi in range(2)]
        sil = abf.alloc("sil", 4 * TC)
        inter = abf.alloc("inter", 4 * TC)
        w_cm = abf.alloc("w_cm", T)
        wt16 = abf.alloc("wt16", 32)
        w_rep = [abf.alloc(f"wrep{i}", TC) for i in range(2)]
        outsb = af32.alloc("outsb", NCH * TC)
        rows = af32.alloc("rowsF", 4 * TC)
        gt = af32.alloc("gt", 64)

        def xwc(c, tcg):
            return xw[:, c * T + tcg * TC:c * T + tcg * TC + TC]

        sd_x = []
        se_sq = []
        pe_ln = []
        ve_var = []
        se_rstd = []
        pe_rep = []
        ve_xln = []
        ve_gt = []
        ve_wcm = []
        for tcg in range(NTC):
            roll = xt16[0]
            w = []
            if tcg >= 1:
                w += [(cVE, ve_xln[tcg - 1]), (cSE, se_sq[tcg - 1]),
                      (cPE, pe_ln[tcg - 1])]
            sd_x.append(P.do(SP, lambda e, tcg=tcg, roll=roll: e.dma_start(
                out=roll.ap([TC, NCH], [1, TC]), in_=xsp_ap(tcg)),
                inc=(cSD, 16), waits=w))
            w = [(cSD, sd_x[tcg])]
            if tcg >= 1:
                w.append((cPE, pe_ln[tcg - 1]))
            tok = None
            for c in range(NCH):
                tok = P.do(SE, lambda e, c=c, roll=roll: e.activation(
                    sq[:, c * TC:(c + 1) * TC], roll[:, c * TC:(c + 1) * TC],
                    AF.Square), inc=cSE if c == NCH - 1 else None,
                    waits=w if c == 0 else ())
            se_sq.append(tok)
            w = [(cSD, sd_x[tcg])]
            if tcg >= 1:
                w.append((cVE, ve_var[tcg - 1]))
            for c in range(NCH):
                P.do(PE, lambda e, c=c, roll=roll: e.matmul(
                    pb[0][0:1, :], ones_b[:, 0:1], roll[:, c * TC:(c + 1) * TC],
                    start=(c == 0), stop=(c == NCH - 1)), waits=w if c == 0 else ())
            for c in range(NCH):
                tok = P.do(PE, lambda e, c=c: e.matmul(
                    pb[1][0:1, :], ones_b[:, 0:1], sq[:, c * TC:(c + 1) * TC],
                    start=(c == 0), stop=(c == NCH - 1)),
                    inc=cPE if c == NCH - 1 else None,
                    waits=[(cSE, se_sq[tcg])] if c == 0 else ())
            pe_ln.append(tok)
            w = [(cPE, pe_ln[tcg])]
            P.do(VE, lambda e: e.tensor_scalar_mul(
                rows[0:1, 0:TC], pb[0][0:1, :], -1.0 / DM), waits=w)
            P.do(VE, lambda e: e.tensor_scalar_mul(
                rows[0:1, TC:2 * TC], pb[1][0:1, :], 1.0 / DM))
            P.do(VE, lambda e: e.tensor_mul(
                rows[0:1, 2 * TC:3 * TC], rows[0:1, 0:TC], rows[0:1, 0:TC]),
                drain=True)
            ve_var.append(P.do(VE, lambda e: e.tensor_sub(
                rows[0:1, 2 * TC:3 * TC], rows[0:1, TC:2 * TC],
                rows[0:1, 2 * TC:3 * TC]), inc=cVE, drain=True))
            P.do(SE, lambda e: e.activation(
                rows[0:1, 2 * TC:3 * TC], rows[0:1, 2 * TC:3 * TC], AF.Ln,
                bias=eps_s[0:1, 0:1]), waits=[(cVE, ve_var[tcg])])
            se_rstd.append(P.do(SE, lambda e: e.activation(
                rows[0:1, 3 * TC:4 * TC], rows[0:1, 2 * TC:3 * TC], AF.Exp,
                scale=-0.5), inc=cSE, drain=True))
            wrep_ = [(cSE, se_rstd[tcg])]
            if tcg >= 1:
                wrep_.append((cVE, ve_xln[tcg - 1]))
            P.do(PE, lambda e: e.matmul(
                pb[2][:], ones_f[0:1, :], rows[0:1, 0:TC], start=True, stop=True),
                waits=wrep_)
            pe_rep.append(P.do(PE, lambda e: e.matmul(
                pb[3][:], ones_f[0:1, :], rows[0:1, 3 * TC:4 * TC],
                start=True, stop=True), inc=cPE))
            P.do(VE, lambda e: e.tensor_copy(nmrep[:, :], pb[2][:]),
                 waits=[(cPE, pe_rep[tcg])])
            P.do(VE, lambda e: e.tensor_copy(rsrep[:, :], pb[3][:]))
            for c in range(NCH):
                P.do(VE, lambda e, c=c, roll=roll: e.tensor_add(
                    ttmp[:, :], roll[:, c * TC:(c + 1) * TC], nmrep[:, :]),
                    drain=True)
                tok = P.do(VE, lambda e, c=c, tcg=tcg: e.tensor_mul(
                    xwc(c, tcg), ttmp[:, :], rsrep[:, :]),
                    inc=cVE if c == NCH - 1 else None, drain=True)
            ve_xln.append(tok)
            for tt in range(4):
                w = [(cVE, ve_xln[tcg])]
                if ve_gt:
                    w.append((cVE, ve_gt[-1]))
                for kc in range(NCH):
                    tok = P.do(PE, lambda e, kc=kc, tcg=tcg, tt=tt: e.matmul(
                        pb[4][:, 0:8],
                        xw[:, kc * T + tcg * TC + tt * 128:
                           kc * T + tcg * TC + tt * 128 + 128],
                        gw_s[:, kc * 8:(kc + 1) * 8],
                        start=(kc == 0), stop=(kc == NCH - 1)),
                        inc=cPE if kc == NCH - 1 else None,
                        waits=w if kc == 0 else ())
                pe_g = tok
                P.do(VE, lambda e: e.tensor_reduce(
                    gt[:, 16], pb[4][:, 0:8], AX.X, ALU.max),
                    waits=[(cPE, pe_g)], drain=True)
                P.do(VE, lambda e: e.tensor_single_scalar(
                    gt[:, 0:8], pb[4][:, 0:8], gt[:, 16], ALU.is_equal),
                    drain=True)
                P.do(VE, lambda e: e.scalar_tensor_tensor(
                    gt[:, 8:16], gt[:, 0:8], -1e30, pb[4][:, 0:8],
                    ALU.mult, ALU.add), drain=True)
                P.do(VE, lambda e: e.tensor_reduce(
                    gt[:, 17], gt[:, 8:16], AX.X, ALU.max), drain=True)
                t_d21 = P.do(VE, lambda e: e.tensor_sub(
                    gt[:, 18], gt[:, 17], gt[:, 16]), inc=cVE, drain=True)
                t_e2 = P.do(SE, lambda e: e.activation(
                    gt[:, 19], gt[:, 18], AF.Exp), inc=cSE,
                    waits=[(cVE, t_d21)])
                P.do(VE, lambda e: e.tensor_scalar_add(
                    gt[:, 20], gt[:, 19], 1.0), waits=[(cSE, t_e2)], drain=True)
                P.do(VE, lambda e: e.reciprocal(gt[:, 21], gt[:, 20]),
                     drain=True)
                P.do(VE, lambda e: e.tensor_scalar(
                    gt[:, 22], gt[:, 21], -1.0, 1.0, ALU.mult, ALU.add),
                    drain=True)
                P.do(VE, lambda e: e.tensor_single_scalar(
                    gt[:, 24:32], gt[:, 8:16], gt[:, 17], ALU.is_equal),
                    drain=True)
                P.do(VE, lambda e: e.tensor_single_scalar(
                    gt[:, 32:40], gt[:, 24:32], gt[:, 22], ALU.mult),
                    drain=True)
                P.do(VE, lambda e: e.scalar_tensor_tensor(
                    gt[:, 40:48], gt[:, 0:8], gt[:, 21], gt[:, 32:40],
                    ALU.mult, ALU.add), drain=True)
                ve_w = P.do(VE, lambda e: e.tensor_copy(
                    wt16[:, 0:8], gt[:, 40:48]), inc=cVE, drain=True)
                w4 = [(cVE, ve_w)]
                if ve_wcm:
                    w4.append((cVE, ve_wcm[-1]))
                ptok = P.do(PE, lambda e: e.transpose(
                    pbf[0:8, 0:128], wt16[:, 0:8], ident_b[:]),
                    inc=cPE, waits=w4)
                ve_wcm.append(P.do(VE, lambda e, tcg=tcg, tt=tt: e.tensor_copy(
                    w_cm[0:8, tcg * TC + tt * 128:tcg * TC + tt * 128 + 128],
                    pbf[0:8, 0:128]), inc=cVE, waits=[(cPE, ptok)]))
                ve_gt.append(ve_wcm[-1])

        subs = [("s", 0, q) for q in range(4)] + \
               [("e", ee, q) for ee in range(NEXP) for q in range(4)]
        gd_sub = []
        pe_sub_last = []
        ve_h2 = {}
        se_sil = {}
        se_out = {}
        gd_acc = []
        ve_wr = {}
        for si, (kind, ee, qq) in enumerate(subs):
            grp = wg[si % 2]
            w = []
            if si >= 2:
                w.append((cPE, pe_sub_last[si - 2]))
            if si == 0:
                w.append((cVE, ve_xln[0]))
            toks = []
            for j, nm in enumerate(("1", "2", "3")):
                if kind == "s":
                    src = wdr["sw" + nm]
                else:
                    src = wdr["ew" + nm][ee]
                if j < 2:
                    ap_in = src.rearrange("(c p) n -> p c n", p=128)[
                        :, :, qq * 512:(qq + 1) * 512]
                    ap_out = grp[j].ap([512, NCH], [1, 512])
                else:
                    ap_in = src[qq * 512:(qq + 1) * 512].rearrange(
                        "(c p) n -> p c n", p=128)
                    ap_out = grp[j].ap([1024, 4], [1, 1024])
                toks.append(P.do(PO, lambda e, ap_in=ap_in, ap_out=ap_out:
                                 e.dma_start(out=ap_out, in_=ap_in),
                                 inc=(cGD, 16), waits=w if j == 0 else ()))
            gd_sub.append(toks[-1])
            for tcg in range(NTC):
                if kind == "e":
                    wr = w_rep[tcg % 2]
                    w5 = [(cVE, ve_wcm[-1])] if (si <= 4 and tcg == 0) else []
                    if (si, tcg) != (4, 0) and (si > 4 or tcg > 0):
                        prev = ve_wr.get((si, tcg - 2)) or \
                            ve_wr.get((si - 1, NTC - 2 + tcg))
                        if prev:
                            w5.append((cVE, prev))
                    ptok = P.do(PE, lambda e, ee=ee, tcg=tcg: e.matmul(
                        pb[4][:, 0:512], sel8_s[0:8, ee * 128:(ee + 1) * 128],
                        w_cm[0:8, tcg * TC:(tcg + 1) * TC],
                        start=True, stop=True), inc=cPE, waits=w5)
                    ve_wr[(si, tcg)] = P.do(VE, lambda e, wr=wr: e.tensor_copy(
                        wr[:, :], pb[4][:, 0:512]), inc=cVE,
                        waits=[(cPE, ptok)])
                for hc in range(4):
                    bank = pb[hc % 2]
                    w6 = []
                    if si == 0 and tcg == 0 and hc == 0:
                        w6.append((cVE, ve_xln[NTC - 1]))
                    if (si, tcg) != (0, 0) or hc >= 2:
                        key = (si, tcg, hc - 2) if hc >= 2 else \
                            ((si, tcg - 1, 2 + hc) if tcg >= 1 else
                             (si - 1, NTC - 1, 2 + hc))
                        if key in se_sil:
                            w6.append((cSE, se_sil[key]))
                    if si == 0 and tcg == 0 and hc == 0:
                        w6.append((cGD, gd_sub[0]))
                    elif hc == 0 and tcg == 0:
                        w6.append((cGD, gd_sub[si]))
                    for kc in range(NCH):
                        tok = P.do(PE, lambda e, kc=kc, hc=hc, tcg=tcg, grp=grp,
                                   bank=bank: e.matmul(
                            bank[:],
                            grp[0][:, kc * 512 + hc * 128:kc * 512 + hc * 128 + 128],
                            xwc(kc, tcg),
                            start=(kc == 0), stop=(kc == NCH - 1)),
                            inc=cPE if kc == NCH - 1 else None,
                            waits=w6 if kc == 0 else ())
                    se_sil[(si, tcg, hc)] = P.do(SE, lambda e, hc=hc, bank=bank:
                                                 e.activation(
                        sil[:, hc * TC:(hc + 1) * TC], bank[:], AF.Silu),
                        inc=cSE, waits=[(cPE, tok)])
                for hc in range(4):
                    bank = pb[2 + hc % 2]
                    w7 = []
                    key = (si, tcg, hc - 2) if hc >= 2 else \
                        ((si, tcg - 1, 2 + hc) if tcg >= 1 else
                         (si - 1, NTC - 1, 2 + hc))
                    if key in ve_h2:
                        w7.append((cVE, ve_h2[key]))
                    for kc in range(NCH):
                        tok = P.do(PE, lambda e, kc=kc, hc=hc, tcg=tcg, grp=grp,
                                   bank=bank: e.matmul(
                            bank[:],
                            grp[1][:, kc * 512 + hc * 128:kc * 512 + hc * 128 + 128],
                            xwc(kc, tcg),
                            start=(kc == 0), stop=(kc == NCH - 1)),
                            inc=cPE if kc == NCH - 1 else None,
                            waits=w7 if kc == 0 else ())
                    w8 = [(cPE, tok), (cSE, se_sil[(si, tcg, hc)])]
                    ko = (si - 1, tcg) if si >= 1 else None
                    if ko in se_out:
                        w8.append((cSE, se_out[ko]))
                    tfirst = P.do(VE, lambda e, hc=hc, bank=bank:
                                  e.tensor_tensor(
                        inter[:, hc * TC:(hc + 1) * TC],
                        sil[:, hc * TC:(hc + 1) * TC], bank[:], ALU.mult),
                        waits=w8, inc=cVE)
                    ve_h2[(si, tcg, hc)] = tfirst
                    if kind == "e":
                        w9 = [(cVE, ve_wr[(si, tcg)])] if hc == 0 else []
                        P.do(VE, lambda e, hc=hc, wr=w_rep[tcg % 2]:
                             e.tensor_mul(
                            inter[:, hc * TC:(hc + 1) * TC],
                            inter[:, hc * TC:(hc + 1) * TC], wr[:, :]),
                            inc=cVE if hc == 3 else None,
                            waits=w9, drain=True)
                ve_inter = cVE.n
                se_last = None
                for dc in range(NCH):
                    bank = pb[5 + dc % 2]
                    w10 = [(cVE, ve_inter)] if dc == 0 else []
                    if dc == 0 and gd_acc:
                        w10.append((cGD, gd_acc[-1]))
                    if dc < 2:
                        ko2 = (si, tcg - 1) if tcg >= 1 else (si - 1, NTC - 1)
                        if ko2 in se_out:
                            w10.append((cSE, se_out[ko2]))
                    if dc >= 2:
                        w10.append((cSE, se_last_toks[dc - 2]))
                    for kc in range(4):
                        tok = P.do(PE, lambda e, kc=kc, dc=dc, grp=grp,
                                   bank=bank: e.matmul(
                            bank[:],
                            grp[2][:, kc * 1024 + dc * 128:kc * 1024 + dc * 128 + 128],
                            inter[:, kc * TC:(kc + 1) * TC],
                            start=(kc == 0), stop=(kc == 3)),
                            inc=cPE if kc == 3 else None,
                            waits=w10 if kc == 0 else ())
                    if dc == 0:
                        se_last_toks = []
                    se_last_toks.append(P.do(SE, lambda e, dc=dc, bank=bank:
                                             e.copy(
                        outsb[:, dc * TC:(dc + 1) * TC], bank[:]),
                        inc=cSE, waits=[(cPE, tok)]))
                se_out[(si, tcg)] = se_last_toks[-1]
                pe_last_here = cPE.n
                gd_acc.append(P.do(PO, lambda e, tcg=tcg, first=(si == 0):
                                   e.dma_start(
                    out=oac_ap(tcg), in_=outsb.ap([TC, NCH], [1, TC]),
                    accum_op=(ALU.bypass if first else ALU.add)),
                    inc=(cGD, 16), waits=[(cSE, se_out[(si, tcg)])]))
            pe_sub_last.append(cPE.n)

        sd_oi = []
        sd_xi = []
        ve_add = []
        sd_out = []
        for tcg in range(NTC):
            roll = xt16[0]
            w = [(cGD, gd_acc[-1])]
            if sd_out:
                w.append((cSD, sd_out[-1]))
            sd_oi.append(P.do(SP, lambda e, tcg=tcg: e.dma_start(
                out=outsb.ap([TC, NCH], [1, TC]), in_=oac_ap(tcg)),
                inc=(cSD, 16), waits=w))
            w = [(cVE, ve_add[tcg - 1])] if tcg >= 1 else \
                [(cVE, ve_xln[NTC - 1])]
            sd_xi.append(P.do(SP, lambda e, tcg=tcg, roll=roll: e.dma_start(
                out=roll.ap([TC, NCH], [1, TC]), in_=xsp_ap(tcg)),
                inc=(cSD, 16), waits=w))
            tok = None
            for c in range(NCH):
                w11 = [(cSD, sd_oi[tcg]), (cSD, sd_xi[tcg])] if c == 0 else []
                if c == 0 and tcg == 0:
                    w11.append((cSE, se_out[(len(subs) - 1, NTC - 1)]))
                if inv == 0:
                    tok = P.do(VE, lambda e, c=c, roll=roll: e.tensor_add(
                        sq[:, c * TC:(c + 1) * TC],
                        roll[:, c * TC:(c + 1) * TC],
                        outsb[:, c * TC:(c + 1) * TC]),
                        inc=cVE if c == NCH - 1 else None, waits=w11)
                else:
                    tok = P.do(VE, lambda e, c=c, roll=roll: e.tensor_add(
                        outsb[:, c * TC:(c + 1) * TC],
                        roll[:, c * TC:(c + 1) * TC],
                        outsb[:, c * TC:(c + 1) * TC]),
                        inc=cVE if c == NCH - 1 else None, waits=w11)
            ve_add.append(tok)
            if inv == 0:
                sd_out.append(P.do(SP, lambda e, tcg=tcg: e.dma_start(
                    out=xsp_ap(tcg), in_=sq.ap([TC, NCH], [1, TC])),
                    inc=(cSD, 16), waits=[(cVE, tok)]))
            else:
                bq_, tq = divmod(tcg, NTC_B)
                sd_out.append(P.do(SP, lambda e, bq_=bq_, tq=tq: e.dma_start(
                    out=out_d.rearrange("c p b n -> p b c n")[
                        :, bq_, :, tq * TC:(tq + 1) * TC],
                    in_=outsb.ap([TC, NCH], [1, TC])),
                    inc=(cSD, 16), waits=[(cVE, tok)]))

        play(P)
        abf.free(xw, xt16[0], sq, nmrep, rsrep, ttmp, sil, inter,
                 w_cm, wt16, w_rep[0], w_rep[1],
                 *[b_ for gi in range(2) for b_ in wg[gi]])
        af32.free(outsb, rows, gt)

    def phase_ssm():
        P = newplan()
        winb = abf.alloc("winb", NCH * 4096)
        xt16 = [abf.alloc(f"xt16s{i}", NCH * TC) for i in range(2)]
        xln3 = abf.alloc("xln3", NCH * TC)
        sq = abf.alloc("sqS", NCH * TC)
        nmrep = abf.alloc("nmrepS", TC)
        rsrep = abf.alloc("rsrepS", TC)
        ttmp = abf.alloc("ttmpS", TC)
        xpb = abf.alloc("xpb", 16 * (TC + 3) + 32)
        xcb = abf.alloc("xcb", 16 * TC)
        zb = abf.alloc("zb", 16 * TC)
        cacc = af32.alloc("cacc", 2 * TC)
        rows = af32.alloc("rowsS", 4 * TC)

        gd_win = P.do(PO, lambda e: e.dma_start(
            out=winb.ap([4096, NCH], [1, 4096]),
            in_=wdr["W_inT"].rearrange("(c p) n -> p c n", p=128)),
            inc=(cGD, 16))
        sd_x = []
        se_sq = []
        pe_ln = []
        ve_var = []
        se_rstd = []
        pe_rep = []
        ve_xln = []
        ve_conv = []
        se_z = {}
        se_xc = {}
        sd_sp = []
        pe_xz = {}
        for b in range(NB):
            P.do(VE, lambda e, b=b: e.memset(
                xpb.ap([TC + 3, 16], [1, 3]), 0.0), drain=True)
            for tq in range(NTC_B):
                ti = b * NTC_B + tq
                tcg = b * NTC_B + tq
                roll = xt16[ti % 2]
                w = []
                if ti >= 2:
                    w.append((cVE, ve_xln[ti - 2]))
                sd_x.append(P.do(SP, lambda e, tcg=tcg, roll=roll: e.dma_start(
                    out=roll.ap([TC, NCH], [1, TC]), in_=xsp_ap(tcg)),
                    inc=(cSD, 16), waits=w))
                w = [(cSD, sd_x[ti])]
                if ti >= 1:
                    w.append((cPE, pe_ln[ti - 1]))
                tok = None
                for c in range(NCH):
                    tok = P.do(SE, lambda e, c=c, roll=roll: e.activation(
                        sq[:, c * TC:(c + 1) * TC], roll[:, c * TC:(c + 1) * TC],
                        AF.Square), inc=cSE if c == NCH - 1 else None,
                        waits=w if c == 0 else ())
                se_sq.append(tok)
                w = [(cSD, sd_x[ti])]
                if ti >= 1:
                    w.append((cVE, ve_var[ti - 1]))
                for c in range(NCH):
                    P.do(PE, lambda e, c=c, roll=roll: e.matmul(
                        pb[0][0:1, :], ones_b[:, 0:1],
                        roll[:, c * TC:(c + 1) * TC],
                        start=(c == 0), stop=(c == NCH - 1)),
                        waits=w if c == 0 else ())
                for c in range(NCH):
                    tok = P.do(PE, lambda e, c=c: e.matmul(
                        pb[1][0:1, :], ones_b[:, 0:1], sq[:, c * TC:(c + 1) * TC],
                        start=(c == 0), stop=(c == NCH - 1)),
                        inc=cPE if c == NCH - 1 else None,
                        waits=[(cSE, se_sq[ti])] if c == 0 else ())
                pe_ln.append(tok)
                P.do(VE, lambda e: e.tensor_scalar_mul(
                    rows[0:1, 0:TC], pb[0][0:1, :], -1.0 / DM),
                    waits=[(cPE, pe_ln[ti])])
                P.do(VE, lambda e: e.tensor_scalar_mul(
                    rows[0:1, TC:2 * TC], pb[1][0:1, :], 1.0 / DM))
                P.do(VE, lambda e: e.tensor_mul(
                    rows[0:1, 2 * TC:3 * TC], rows[0:1, 0:TC], rows[0:1, 0:TC]),
                    drain=True)
                ve_var.append(P.do(VE, lambda e: e.tensor_sub(
                    rows[0:1, 2 * TC:3 * TC], rows[0:1, TC:2 * TC],
                    rows[0:1, 2 * TC:3 * TC]), inc=cVE, drain=True))
                P.do(SE, lambda e: e.activation(
                    rows[0:1, 2 * TC:3 * TC], rows[0:1, 2 * TC:3 * TC], AF.Ln,
                    bias=eps_s[0:1, 0:1]), waits=[(cVE, ve_var[ti])])
                se_rstd.append(P.do(SE, lambda e: e.activation(
                    rows[0:1, 3 * TC:4 * TC], rows[0:1, 2 * TC:3 * TC], AF.Exp,
                    scale=-0.5), inc=cSE, drain=True))
                wrep_ = [(cSE, se_rstd[ti])]
                if ti >= 1:
                    wrep_.append((cVE, ve_xln[ti - 1]))
                P.do(PE, lambda e: e.matmul(
                    pb[2][:], ones_f[0:1, :], rows[0:1, 0:TC],
                    start=True, stop=True), waits=wrep_)
                pe_rep.append(P.do(PE, lambda e: e.matmul(
                    pb[3][:], ones_f[0:1, :], rows[0:1, 3 * TC:4 * TC],
                    start=True, stop=True), inc=cPE))
                P.do(VE, lambda e: e.tensor_copy(nmrep[:, :], pb[2][:]),
                     waits=[(cPE, pe_rep[ti])])
                P.do(VE, lambda e: e.tensor_copy(rsrep[:, :], pb[3][:]))
                for c in range(NCH):
                    P.do(VE, lambda e, c=c, roll=roll: e.tensor_add(
                        ttmp[:, :], roll[:, c * TC:(c + 1) * TC], nmrep[:, :]),
                        drain=True)
                    tok = P.do(VE, lambda e, c=c: e.tensor_mul(
                        xln3[:, c * TC:(c + 1) * TC], ttmp[:, :], rsrep[:, :]),
                        inc=cVE if c == NCH - 1 else None, drain=True)
                ve_xln.append(tok)
                for ec in range(32):
                    bank = pb[4 + ec % 2]
                    w = [(cVE, ve_xln[ti])] if ec == 0 else []
                    if ec == 0 and ti == 0:
                        w.append((cGD, gd_win))
                    prev = (ti, ec - 2) if ec >= 2 else \
                        ((ti - 1, 30 + ec) if ti >= 1 else None)
                    if prev in se_z:
                        w.append((cSE, se_z[prev]))
                    if prev in se_xc:
                        w.append((cVE if False else cSE, se_xc[prev]))
                    for kc in range(NCH):
                        tok = P.do(PE, lambda e, kc=kc, ec=ec, bank=bank:
                                   e.matmul(
                            bank[:],
                            winb[:, kc * 4096 + ec * 128:kc * 4096 + ec * 128 + 128],
                            xln3[:, kc * TC:(kc + 1) * TC],
                            start=(kc == 0), stop=(kc == NCH - 1)),
                            inc=cPE if kc == NCH - 1 else None,
                            waits=w if kc == 0 else ())
                    pe_xz[(ti, ec)] = tok
                    if ec >= 16:
                        wz = [(cPE, tok)]
                        if ec == 16 and ti >= 1:
                            wz.append((cSD, sd_sp[2 * ti - 2]))
                        se_z[(ti, ec)] = P.do(SE, lambda e, ec=ec, bank=bank:
                                              e.copy(
                            zb[:, (ec - 16) * TC:(ec - 15) * TC], bank[:]),
                            inc=cSE, waits=wz)
                    else:
                        wxc = [(cPE, tok)]
                        if ec == 0 and ti >= 1:
                            wxc.append((cSD, sd_sp[2 * ti - 1]))
                        se_xc[(ti, ec)] = P.do(SE, lambda e, ec=ec, bank=bank:
                                               e.copy(
                            bass.AP(abf_t, xpb.off + ec * (TC + 3) + 3,
                                    [[ABF, 128], [1, TC]]), bank[:]),
                            inc=cSE, waits=wxc)
                        woff = xpb.off + ec * (TC + 3)
                        w12 = [(cSE, se_xc[(ti, ec)])]
                        pec = (ti, ec - 1) if ec >= 1 else \
                            ((ti - 1, 15) if ti >= 1 else None)
                        if pec in se_xc:
                            w12.append((cSE, se_xc[pec]))
                        P.do(VE, lambda e, woff=woff, ec=ec: e.tensor_single_scalar(
                            cacc[:, 0:TC],
                            bass.AP(abf_t, woff, [[ABF, 128], [1, TC]]),
                            cw_s[:, ec * 4:ec * 4 + 1], ALU.mult), waits=w12)
                        for jj in range(1, 4):
                            P.do(VE, lambda e, woff=woff, ec=ec, jj=jj:
                                 e.scalar_tensor_tensor(
                                cacc[:, 0:TC],
                                bass.AP(abf_t, woff + jj, [[ABF, 128], [1, TC]]),
                                cw_s[:, ec * 4 + jj:ec * 4 + jj + 1],
                                cacc[:, 0:TC], ALU.mult, ALU.add), drain=True)
                        tok2 = P.do(VE, lambda e, woff=woff: e.tensor_copy(
                            bass.AP(abf_t, woff, [[ABF, 128], [1, 3]]),
                            bass.AP(abf_t, woff + TC, [[ABF, 128], [1, 3]])),
                            inc=cVE, drain=True)
                        ve_conv.append(tok2)
                        se_xc[(ti, ec)] = P.do(SE, lambda e, ec=ec: e.activation(
                            xcb[:, ec * TC:(ec + 1) * TC], cacc[:, 0:TC],
                            AF.Silu, bias=cb_s[:, ec:ec + 1]), inc=cSE,
                            waits=[(cVE, tok2)])
                w13 = [(cSE, se_z[(ti, 31)])]
                if len(sd_sp) >= 2:
                    w13.append((cSD, sd_sp[-2]))
                sd_sp.append(P.do(SP, lambda e, b=b, tq=tq: e.dma_start(
                    out=z_d[b, :, :, tq * TC:(tq + 1) * TC],
                    in_=zb.ap([TC, 16], [1, TC])), inc=(cSD, 16), waits=w13))
                sd_sp.append(P.do(SP, lambda e, b=b, tq=tq: e.dma_start(
                    out=xc_d[b, :, :, tq * TC:(tq + 1) * TC],
                    in_=xcb.ap([TC, 16], [1, TC])), inc=(cSD, 16),
                    waits=[(cSE, se_xc[(ti, 15)])]))
        play(P)
        abf.free(winb, xt16[0], xt16[1], xln3, sq, nmrep, rsrep, ttmp,
                 xpb, xcb, zb)
        af32.free(cacc, rows)

        P = newplan()
        xcb2 = [abf.alloc("xcb2A", 16 * TC)]
        xcb2.append(xcb2[0])
        dbrep = abf.alloc("dbrep", 16 * TC)
        crep = abf.alloc("crep", 16 * TC)
        hbuf = abf.alloc("hbuf", 16 * TC)
        yall = abf.alloc("yall", 16 * TC)
        zb2 = abf.alloc("zb2", 16 * TC)
        zs16 = abf.alloc("zs16", TC)
        woutb = abf.alloc("woutb", 16 * 1024)
        w16 = [abf.alloc(f"w16{i}", TC) for i in range(2)]
        bcsb = abf.alloc("bcsb", TC)
        xres = abf.alloc("xres", NCH * TC)
        state = af32.alloc("state", 256)
        da16 = [af32.alloc(f"daf{i}", TC) for i in range(2)]
        bcf = af32.alloc("bcf", 2 * TC)
        drep = af32.alloc("drep", TC)
        ytmp = af32.alloc("ytmp", TC)

        gd_wout = P.do(PO, lambda e: e.dma_start(
            out=woutb.ap([1024, 16], [1, 1024]),
            in_=wdr["W_outT"].rearrange("(c p) n -> p c n", p=128)),
            inc=(cGD, 16))

        sd_xc = []
        sd_z = []
        sd_xr = []
        sd_xo = []
        ve_fin = []
        for b in range(NB):
            P.do(VE, lambda e: e.memset(state[:, :], 0.0))
            ve_st0 = P.do(VE, lambda e: e.engine_nop(), inc=cVE)
            for tq in range(NTC_B):
                ti = b * NTC_B + tq
                tcg = b * NTC_B + tq
                xc2 = xcb2[0]
                w = []
                if ti >= 1:
                    w.append((cVE, ve_fin[ti - 1][0]))
                sd_xc.append(P.do(SP, lambda e, b=b, tq=tq, xc2=xc2:
                                  e.dma_start(
                    out=xc2.ap([TC, 16], [1, TC]),
                    in_=xc_d[b, :, :, tq * TC:(tq + 1) * TC]),
                    inc=(cSD, 16), waits=w))
                w = []
                if ti >= 1:
                    w.append((cVE, ve_fin[ti - 1][1]))
                sd_z.append(P.do(SP, lambda e, b=b, tq=tq: e.dma_start(
                    out=zb2.ap([TC, 16], [1, TC]),
                    in_=z_d[b, :, :, tq * TC:(tq + 1) * TC]),
                    inc=(cSD, 16), waits=w))
                w = []
                if ti >= 1:
                    w.append((cVE, ve_fin[ti - 1][2]))
                    w.append((cSD, sd_xo[-1]))
                sd_xr.append(P.do(SP, lambda e, tcg=tcg: e.dma_start(
                    out=xres.ap([TC, NCH], [1, TC]), in_=xsp_ap(tcg)),
                    inc=(cSD, 16), waits=w))
                w = [(cSD, sd_xc[ti])]
                if ti >= 1:
                    w.append((cVE, ve_bc))
                for kc in range(16):
                    tok = P.do(PE, lambda e, kc=kc, xc2=xc2: e.matmul(
                        pb[0][0:80, :], wx_s[:, kc * 80:(kc + 1) * 80],
                        xc2[:, kc * TC:(kc + 1) * TC],
                        start=(kc == 0), stop=(kc == 15)),
                        inc=cPE if kc == 15 else None,
                        waits=w if kc == 0 else ())
                ve_wx = P.do(VE, lambda e: e.tensor_copy(
                    bcf[0:80, 0:TC], pb[0][0:80, :]), inc=cVE,
                    waits=[(cPE, tok)])
                se_d = P.do(SE, lambda e: e.activation(
                    bcf[0:1, TC:2 * TC], bcf[0:1, 0:TC], AF.Abs),
                    waits=[(cVE, ve_wx)], drain=True)
                P.do(SE, lambda e: e.activation(
                    bcf[0:1, TC:2 * TC], bcf[0:1, TC:2 * TC], AF.Exp,
                    scale=-1.0), drain=True)
                se_d = P.do(SE, lambda e: e.activation(
                    bcf[0:1, TC:2 * TC], bcf[0:1, TC:2 * TC], AF.Ln, bias=1.0),
                    inc=cSE, drain=True)
                P.do(VE, lambda e: e.tensor_scalar_max(
                    drep[0:1, 0:TC], bcf[0:1, 0:TC], 0.0),
                    waits=[(cSE, se_d)], drain=True)
                ve_dl = P.do(VE, lambda e: e.tensor_add(
                    drep[0:1, 0:TC], drep[0:1, 0:TC], bcf[0:1, TC:2 * TC]),
                    inc=cVE, drain=True)
                pe_dr = P.do(PE, lambda e: e.matmul(
                    pb[1][:], ones_f[0:1, :], drep[0:1, 0:TC],
                    start=True, stop=True), inc=cPE, waits=[(cVE, ve_dl)])
                ve_drep = P.do(VE, lambda e: e.tensor_copy(
                    drep[:, :], pb[1][:]), inc=cVE, waits=[(cPE, pe_dr)])
                P.do(VE, lambda e: e.tensor_mul(
                    bcsb[32:48, 0:TC], bcf[32:48, 0:TC], drep[32:48, 0:TC]),
                    waits=[(cVE, ve_drep)], drain=True)
                ve_bc = P.do(VE, lambda e: e.tensor_copy(
                    bcsb[64:80, 0:TC], bcf[64:80, 0:TC]), inc=cVE, drain=True)
                pe_rep_t = []
                for s in range(16):
                    w14 = [(cVE, ve_bc)] if s == 0 else []
                    if len(pe_rep_t) >= 2:
                        w14.append((cVE, ve_reps[s - 2]))
                    bank = pb[2 + s % 2]
                    ptok = P.do(PE, lambda e, s=s, bank=bank: e.matmul(
                        bank[:], sel16_s[32:48, s * 128:(s + 1) * 128],
                        bcsb[32:48, 0:TC], start=True, stop=True),
                        inc=cPE, waits=w14)
                    pe_rep_t.append(ptok)
                    if s == 0:
                        ve_reps = []
                    ve_reps.append(P.do(VE, lambda e, s=s, bank=bank:
                                        e.tensor_copy(
                        dbrep[:, s * TC:(s + 1) * TC], bank[:]), inc=cVE,
                        waits=[(cPE, ptok)]))
                for s in range(16):
                    w14 = []
                    if len(pe_rep_t) >= 2:
                        w14.append((cVE, ve_reps[14 + s]))
                    bank = pb[2 + s % 2]
                    ptok = P.do(PE, lambda e, s=s, bank=bank: e.matmul(
                        bank[:], sel16_s[64:80, s * 128:(s + 1) * 128],
                        bcsb[64:80, 0:TC], start=True, stop=True),
                        inc=cPE, waits=w14)
                    pe_rep_t.append(ptok)
                    ve_reps.append(P.do(VE, lambda e, s=s, bank=bank:
                                        e.tensor_copy(
                        crep[:, s * TC:(s + 1) * TC], bank[:]), inc=cVE,
                        waits=[(cPE, ptok)]))
                ve_crep_done = ve_reps[-1]
                se_da = {}
                ve_scan = {}
                for dchunk in range(16):
                    for s in range(16):
                        ii = dchunk * 16 + s
                        par = ii % 2
                        w15 = [(cVE, ve_drep)] if ii == 0 else []
                        if ii >= 2:
                            w15.append((cVE, ve_scan.get(ii - 2, 0)))
                        se_da[ii] = P.do(SE, lambda e, dchunk=dchunk, s=s,
                                         par=par: e.activation(
                            da16[par][:, :], drep[:, :], AF.Exp,
                            scale=acol_s[:, dchunk * 16 + s:dchunk * 16 + s + 1]),
                            inc=cSE, waits=w15)
                        w16_ = [(cVE, ve_crep_done)] if ii == 0 else []
                        if ii >= 2:
                            w16_.append((cVE, ve_scan.get(ii - 2, 0)))
                        P.do(VE, lambda e, dchunk=dchunk, s=s, par=par, xc2=xc2:
                             e.tensor_mul(
                            w16[par][:, :], xc2[:, dchunk * TC:(dchunk + 1) * TC],
                            dbrep[:, s * TC:(s + 1) * TC]), waits=w16_)
                        w17 = [(cSE, se_da[ii])]
                        if ii == 0:
                            w17.append((cVE, ve_st0))
                        P.do(VE, lambda e, s=s, par=par, dchunk=dchunk:
                             e.tensor_tensor_scan(
                            hbuf[:, s * TC:(s + 1) * TC], da16[par][:, :],
                            w16[par][:, :],
                            state[:, dchunk * 16 + s:dchunk * 16 + s + 1],
                            ALU.mult, ALU.add), waits=w17, drain=True)
                        ve_scan[ii] = P.do(VE, lambda e, s=s, dchunk=dchunk:
                                           e.tensor_copy(
                            state[:, dchunk * 16 + s:dchunk * 16 + s + 1],
                            bass.AP(abf_t, hbuf.off + s * TC + TC - 1,
                                    [[ABF, 128], [1, 1]])), inc=cVE,
                            drain=True)
                    P.do(VE, lambda e: e.tensor_mul(
                        hbuf[:, :], hbuf[:, :], crep[:, :]), drain=True)
                    P.do(VE, lambda e: e.tensor_reduce(
                        ytmp[:, 0:TC],
                        bass.AP(abf_t, hbuf.off,
                                [[ABF, 128], [1, TC], [TC, 16]]),
                        AX.X, ALU.add), drain=True)
                    w18 = [(cSD, sd_z[ti])]
                    if dchunk >= 1:
                        w18.append((cVE, ve_scan[dchunk * 16 - 1]))
                    se_zs = P.do(SE, lambda e, dchunk=dchunk: e.activation(
                        zs16[:, :], zb2[:, dchunk * TC:(dchunk + 1) * TC],
                        AF.Silu), inc=cSE, waits=w18)
                    tok = P.do(VE, lambda e, dchunk=dchunk: e.tensor_mul(
                        yall[:, dchunk * TC:(dchunk + 1) * TC], ytmp[:, 0:TC],
                        zs16[:, :]), inc=cVE, waits=[(cSE, se_zs)], drain=True)
                    ve_scan[dchunk * 16 + 15] = tok
                ve_y = tok
                toks_fin = []
                for fc in range(NCH):
                    bank = pb[4 + fc % 2]
                    w19 = [(cVE, ve_y), (cSD, sd_xr[ti])]
                    if fc == 0 and ti == 0:
                        w19.append((cGD, gd_wout))
                    if fc >= 2:
                        w19.append((cVE, toks_fin[fc - 2]))
                    for kc in range(16):
                        tok = P.do(PE, lambda e, kc=kc, fc=fc, bank=bank:
                                   e.matmul(
                            bank[:],
                            woutb[:, kc * 1024 + fc * 128:kc * 1024 + fc * 128 + 128],
                            yall[:, kc * TC:(kc + 1) * TC],
                            start=(kc == 0), stop=(kc == 15)),
                            inc=cPE if kc == 15 else None,
                            waits=w19 if kc == 0 else ())
                    toks_fin.append(P.do(VE, lambda e, fc=fc, bank=bank:
                                         e.tensor_add(
                        xres[:, fc * TC:(fc + 1) * TC],
                        xres[:, fc * TC:(fc + 1) * TC], bank[:]),
                        inc=cVE, waits=[(cPE, tok)]))
                w20 = [(cVE, toks_fin[-1])]
                if len(sd_xo) >= 1:
                    w20.append((cSD, sd_xo[-1]))
                sd_xo.append(P.do(SP, lambda e, tcg=tcg: e.dma_start(
                    out=xsp_ap(tcg), in_=xres.ap([TC, NCH], [1, TC])),
                    inc=(cSD, 16), waits=w20))
                ve_fin.append((ve_scan[255], cVE.n, toks_fin[-1]))
        play(P)
        abf.free(xcb2[0], dbrep, crep, hbuf, yall, zb2, zs16, woutb,
                 w16[0], w16[1], bcsb, xres)
        af32.free(state, bcf, drep, ytmp, da16[0], da16[1])

    if upto in ("ffn1", "ssm", "full"):
        phase_ffn(0)
    if upto in ("ssm", "full"):
        phase_ssm()
    if upto == "full":
        phase_ffn(1)

    if upto == "full":
        P = None
    else:
        P = newplan()
    ostg = af32.alloc("ostg", NCH * TC) if P else None
    o16 = abf.alloc("o16", NCH * TC) if P else None
    sd_fin = []
    sd_ld = []
    ve_o = []
    for tcg in range(NTC if P else 0):
        bq_, tq = divmod(tcg, NTC_B)
        w = [(cVE, ve_o[tcg - 1])] if tcg >= 1 else []
        sd_ld.append(P.do(SP, lambda e, bq_=bq_, tq=tq: e.dma_start(
            out=o16.ap([TC, NCH], [1, TC]),
            in_=bass.AP(xsp.tensor, bq_ * S + tq * TC,
                        [[NCH * T, 128], [T, NCH], [1, TC]])),
            inc=(cSD, 16), waits=w))
        w = [(cSD, sd_ld[tcg])]
        if tcg >= 2:
            w.append((cSD, sd_fin[tcg - 2]))
        tok = None
        for c in range(NCH):
            tok = P.do(VE, lambda e, c=c: e.tensor_copy(
                ostg[:, c * TC:(c + 1) * TC], o16[:, c * TC:(c + 1) * TC]),
                inc=cVE if c == NCH - 1 else None, waits=w if c == 0 else ())
        ve_o.append(tok)
        sd_fin.append(P.do(SP, lambda e, bq_=bq_, tq=tq: e.dma_start(
            out=out_d.rearrange("c p b n -> p b c n")[:, bq_, :,
                                                      tq * TC:(tq + 1) * TC],
            in_=ostg.ap([TC, NCH], [1, TC])),
            inc=(cSD, 16), waits=[(cVE, tok)]))
    if P:
        play(P)
        af32.free(ostg)
        abf.free(o16)

    Pf = newplan()
    fin_sd, fin_gd = cSD.n, cGD.n
    Pf.do(SP, lambda e: e.nop(), waits=[(cSD, fin_sd)])
    Pf.do(PO, lambda e: e.nop(), waits=[(cGD, fin_gd)])
    Pf.play(nc, None)

    g.close()
    return nc


def _assemble(res):
    outs = []
    for i in range(8):
        a = np.asarray(res.results[i]["xout"], dtype=np.float32)
        outs.append(a.reshape(DM, NB, S).transpose(1, 2, 0))
    return np.concatenate(outs, axis=0)


def kernel(**inputs):
    import os
    upto = os.environ.get("KUPTO", "full")
    if "nc" not in _CACHE:
        _CACHE["nc"] = build_nc(upto=upto)
    nc = _CACHE["nc"]
    host = _host_layout(inputs)
    consts = _host_constants()
    in_maps = []
    for i in range(8):
        m = {"xin": host["_x_percore"][i]}
        for nm in ("wq", "wk", "wv", "wo", "wfq", "wfg", "sw1", "sw2", "sw3",
                   "gate_w", "ew1", "ew2", "ew3", "W_inT", "W_outT", "W_xTp",
                   "conv_w", "conv_b", "A_log", "bq", "bk", "bv", "bo", "bfq",
                   "bfg", "U_pz"):
            m[nm] = host[nm]
        m.update(consts)
        in_maps.append(m)
    res = run_bass_kernel_spmd(nc, in_maps, core_ids=list(range(8)))
    return _assemble(res)


# version 02 (final)
# speedup vs baseline: 1.0403x; 1.0403x over previous
import numpy as np
import ml_dtypes
import concourse.bass as bass
import concourse.mybir as mybir
from concourse.bass_utils import run_bass_kernel_spmd
from contextlib import ExitStack

F32 = mybir.dt.float32
BF16 = mybir.dt.bfloat16
AF = mybir.ActivationFunctionType
ALU = mybir.AluOpType
AX = mybir.AxisListType

DM = 1024
NCH = 8
S = 2048
NB = 2
T = NB * S
H = 16
DH = 64
TC = 512
NTC_B = S // TC
NTC = T // TC
DI = 2048
DS = 16
NEXP = 8
HF = 2048

BF16NP = ml_dtypes.bfloat16
_CACHE = {}

SP, PE, VE, SE, PO = "sync", "tensor", "vector", "scalar", "gpsimd"
ENGS = (SP, PE, VE, SE, PO)


class Ctr:
    def __init__(self, stack, nc, name, dma=False):
        self.h = stack.enter_context(nc.semaphore(name))
        self.n = 0
        self.dma = dma

    def take(self, k=1):
        self.n += k
        return self.n


class Plan:
    def __init__(self, barctr, compctr):
        self.ops = {e: [] for e in ENGS}
        self.barctr = barctr
        self.compctr = compctr

    def do(self, eng, build, inc=None, waits=(), drain=False):
        tok = None
        incs = []
        is_dma = False
        if inc is not None:
            ctr, k = inc if isinstance(inc, tuple) else (inc, 1)
            tok = ctr.take(k)
            incs.append((ctr, k))
            is_dma = ctr.dma
        wl = []
        if is_dma:
            prev = tok - k if inc is not None else None
            if prev:
                wl.append((ctr.h, prev))
        for (c, v) in waits:
            if not v or v <= 0:
                continue
            wl.append((c.h, v))
        self.ops[eng].append({
            "build": build,
            "incs": incs,
            "waits": wl,
            "drain": drain,
            "dma": is_dma,
        })
        return tok

    def play(self, nc, start_snap):
        for e in ENGS:
            lst = self.ops[e]
            if not lst:
                continue
            last = lst[-1]
            if last["dma"]:
                ctr = self.barctr[e] if self.barctr[e].dma else None
                if ctr is None:
                    continue
            else:
                ctr = self.compctr[e]
            if not any(c is ctr for (c, _k) in last["incs"]):
                k = 16 if ctr.dma else 1
                ctr.take(k)
                last["incs"].append((ctr, k))
        with nc.Block() as blk:
            for e in ENGS:
                lst = self.ops[e]
                if not lst:
                    continue

                def run(engine, lst=lst):
                    if start_snap is not None:
                        for (h, v) in start_snap:
                            if v > 0:
                                engine.wait_ge(h, v)
                    for ent in lst:
                        for (h, v) in ent["waits"]:
                            engine.wait_ge(h, v)
                        if ent["drain"]:
                            engine.drain()
                        inst = ent["build"](engine)
                        for (c, k) in ent["incs"]:
                            inst.then_inc(c.h, k)
                getattr(blk, e)(run)


def _to_bf(x):
    return np.ascontiguousarray(np.asarray(x).astype(BF16NP))


class Arena:
    def __init__(self, t, total, name):
        self.t = t
        self.total = total
        self.used = []
        self.hw = 0
        self.name = name

    A = 32

    def alloc(self, name, n):
        nr = -(-n // self.A) * self.A
        self.used.sort()
        pos = 0
        for (s, e, _nm) in self.used:
            if s - pos >= nr:
                break
            pos = -(-max(pos, e) // self.A) * self.A
        assert pos + nr <= self.total, \
            f"{self.name} OOM: {name} needs {nr} at {pos} > {self.total}"
        self.used.append((pos, pos + nr, name))
        self.hw = max(self.hw, pos + nr)
        return Buf(self.t, pos, n, self.total, name)

    def alloc_high(self, name, n):
        nr = -(-n // self.A) * self.A
        self.used.sort()
        pos = (self.total - nr) // self.A * self.A
        for (s, e, _nm) in reversed(self.used):
            if pos >= e:
                break
            pos = min(pos, (s - nr) // self.A * self.A)
        assert pos >= 0, f"{self.name} OOM(high): {name} needs {nr}"
        self.used.append((pos, pos + nr, name))
        self.hw = max(self.hw, pos + nr)
        return Buf(self.t, pos, n, self.total, name)

    def free(self, *bufs):
        for buf in bufs:
            for i, (s, e, nm) in enumerate(self.used):
                if s == buf.off and nm == buf.name:
                    self.used.pop(i)
                    break
            else:
                raise KeyError(f"free unknown {buf.name}")


class Buf:
    def __init__(self, t, off, n, rowlen, name):
        self.t, self.off, self.n, self.rowlen, self.name = t, off, n, rowlen, name

    def __getitem__(self, idx):
        ps, cs = idx
        if isinstance(cs, int):
            cs = slice(cs, cs + 1)
        c0 = cs.start or 0
        c1 = cs.stop if cs.stop is not None else self.n
        assert 0 <= c0 <= c1 <= self.n, f"{self.name}[{c0}:{c1}] of {self.n}"
        return self.t[ps, self.off + c0:self.off + c1]

    def ap(self, *dims, off=0):
        return bass.AP(self.t, self.off + off,
                       [[self.rowlen, 128]] + [list(d) for d in dims])


def _host_constants():
    c = {}
    c["ident_f32"] = np.eye(128, dtype=np.float32)
    c["ident_bf"] = _to_bf(np.eye(128))
    j = np.arange(S, dtype=np.float64) - 1024.0
    j2 = j * j
    j2h = j2.astype(BF16NP).astype(np.float64)
    j2l = j2 - j2h
    jh = j.astype(BF16NP).astype(np.float64)
    jl = j - jh
    rows = np.zeros((32, S), dtype=np.float32)
    rows[0], rows[1], rows[2], rows[3], rows[4] = j2h, j2l, jh, jl, 1.0
    jaux = np.zeros((128, S), dtype=np.float32)
    for slot in range(4):
        jaux[slot * 32:slot * 32 + 32] = rows
    c["jaux"] = _to_bf(jaux)
    sel = np.zeros((128, 16 * 128), dtype=np.float32)
    for s in range(16):
        for base in (0, 32, 64, 96):
            sel[base + s, s * 128:(s + 1) * 128] = 1.0
    c["sel16"] = _to_bf(sel)
    sel8 = np.zeros((8, 8 * 128), dtype=np.float32)
    for e in range(8):
        sel8[e, e * 128:(e + 1) * 128] = 1.0
    c["sel8"] = _to_bf(sel8)
    usq = np.zeros((128, NCH * 16), dtype=np.float32)
    for ch in range(NCH):
        for p in range(128):
            usq[p, ch * 16 + (ch * 128 + p) // DH] = 1.0
    c["usq"] = _to_bf(usq)
    return c


def _host_layout(inp):
    d = {}
    x = np.asarray(inp["L0_attn_a_hidden_states"])
    xt = np.ascontiguousarray(x.transpose(2, 0, 1))
    d["_x_percore"] = [np.ascontiguousarray(xt[:, 2 * i:2 * i + 2, :])
                       for i in range(8)]
    ren = {"wq": "L0_attn_a_wq", "wk": "L0_attn_a_wk", "wv": "L0_attn_a_wv",
           "wo": "L0_attn_a_wo", "wfq": "L0_attn_a_wfq", "wfg": "L0_attn_a_wfg",
           "sw1": "L0_ffn_sw1", "sw2": "L0_ffn_sw2", "sw3": "L0_ffn_sw3",
           "gate_w": "L0_ffn_gate_w", "ew1": "L0_ffn_ew1", "ew2": "L0_ffn_ew2",
           "ew3": "L0_ffn_ew3"}
    for k, v in ren.items():
        d[k] = np.ascontiguousarray(np.asarray(inp[v], dtype=np.float32))
    d["W_inT"] = np.ascontiguousarray(np.asarray(inp["L0_attn_b_W_in"]).T)
    d["W_outT"] = np.ascontiguousarray(np.asarray(inp["L0_attn_b_W_out"]).T)
    wx = np.asarray(inp["L0_attn_b_W_x"])
    wxp = np.zeros((80, DI), dtype=np.float32)
    wxp[0] = wx[0]
    wxp[32:48] = wx[1:17]
    wxp[64:80] = wx[17:33]
    d["W_xTp"] = np.ascontiguousarray(wxp.T)
    cw = np.asarray(inp["L0_attn_b_conv_w"]).reshape(DI, 4)
    d["conv_w"] = np.ascontiguousarray(
        cw.reshape(16, 128, 4).transpose(1, 0, 2).reshape(128, 64))
    d["conv_b"] = np.ascontiguousarray(
        np.asarray(inp["L0_attn_b_conv_b"]).reshape(16, 128).T)
    d["A_log"] = np.ascontiguousarray(
        np.asarray(inp["L0_attn_b_A_log"]).reshape(16, 128, 16)
        .transpose(1, 0, 2).reshape(128, 256))
    for short, nm in (("bq", "L0_attn_a_bq"), ("bk", "L0_attn_a_bk"),
                      ("bv", "L0_attn_a_bv"), ("bo", "L0_attn_a_bo"),
                      ("bfq", "L0_attn_a_bfq"), ("bfg", "L0_attn_a_bfg")):
        d[short] = np.ascontiguousarray(np.asarray(inp[nm]).reshape(8, 128).T)
    up = np.asarray(inp["L0_attn_a_up"]).reshape(H, DH)
    uz = np.asarray(inp["L0_attn_a_uz"]).reshape(H, DH)
    upz = np.zeros((128, NCH * 32), dtype=np.float32)
    for ch in range(NCH):
        for p in range(128):
            dm = ch * 128 + p
            upz[p, ch * 32 + dm // DH] = up[dm // DH, dm % DH]
            upz[p, ch * 32 + 16 + dm // DH] = uz[dm // DH, dm % DH]
    d["U_pz"] = upz
    return d


def build_nc(upto="attn", races=True):
    nc = bass.Bass("TRN2", target_bir_lowering=False,
                   detect_race_conditions=races)
    g = ExitStack()

    xin = nc.dram_tensor("xin", [DM, NB, S], F32, kind="ExternalInput").ap()
    out_d = nc.dram_tensor("xout", [NCH, 128, NB, S], F32,
                           kind="ExternalOutput").ap()
    wdr = {}
    for nm, shp in (
        ("wq", [DM, DM]), ("wk", [DM, DM]), ("wv", [DM, DM]), ("wo", [DM, DM]),
        ("wfq", [DM, DM]), ("wfg", [DM, DM]),
        ("sw1", [DM, HF]), ("sw2", [DM, HF]), ("sw3", [HF, DM]),
        ("gate_w", [DM, NEXP]),
        ("ew1", [NEXP, DM, HF]), ("ew2", [NEXP, DM, HF]), ("ew3", [NEXP, HF, DM]),
        ("W_inT", [DM, 2 * DI]), ("W_outT", [DI, DM]), ("W_xTp", [DI, 80]),
        ("conv_w", [128, 64]), ("conv_b", [128, 16]), ("A_log", [128, 256]),
        ("bq", [128, 8]), ("bk", [128, 8]), ("bv", [128, 8]), ("bo", [128, 8]),
        ("bfq", [128, 8]), ("bfg", [128, 8]), ("U_pz", [128, NCH * 32]),
        ("ident_f32", [128, 128]),
    ):
        wdr[nm] = nc.dram_tensor(nm, shp, F32, kind="ExternalInput").ap()
    for nm, shp in (("ident_bf", [128, 128]), ("jaux", [128, S]),
                    ("sel16", [128, 16 * 128]), ("sel8", [8, 8 * 128]),
                    ("usq", [128, NCH * 16])):
        wdr[nm] = nc.dram_tensor(nm, shp, BF16, kind="ExternalInput").ap()

    xsp = nc.dram_tensor("xspill", [128, NCH * T], BF16, kind="Internal").ap()
    oac = nc.dram_tensor("outacc", [128, NCH * T], F32, kind="Internal").ap()
    v_dram = nc.dram_tensor("vscr", [NB, 128, 16 * 1040], BF16, kind="Internal").ap()
    z_d = nc.dram_tensor("zscr", [NB, 128, 16, S], BF16, kind="Internal").ap()
    xc_d = nc.dram_tensor("xcscr", [NB, 128, 16, S], BF16, kind="Internal").ap()

    sb = {}

    def sbt(name, shape, dt):
        sb[name] = g.enter_context(nc.sbuf_tensor(name, shape, dt))
        return sb[name]

    ones_f = sbt("ones_f", [128, 128], F32)
    ones_b = sbt("ones_b", [128, 128], BF16)
    ident_f = sbt("ident_f", [128, 128], F32)
    ident_b = sbt("ident_b", [128, 128], BF16)
    jaux_s = sbt("jaux_s", [128, S], BF16)
    sel16_s = sbt("sel16_s", [128, 16 * 128], BF16)
    sel8_s = sbt("sel8_s", [8, 8 * 128], BF16)
    usq_s = sbt("usq_s", [128, NCH * 16], BF16)
    upz_s = sbt("upz_s", [128, NCH * 32], BF16)
    bias_s = sbt("bias_s", [128, 48], F32)
    BQ, BK, BV, BO, BFQG, BOBV = 0, 8, 16, 24, 32, 40
    bv16_s = sbt("bv16_s", [128, 8], BF16)
    eps_s = sbt("eps_s", [128, 1], F32)
    acol_s = sbt("acol_s", [128, 256], F32)
    cw_s = sbt("cw_s", [128, 64], F32)
    cb_s = sbt("cb_s", [128, 16], F32)
    wx_s = sbt("wx_s", [128, 16 * 80], BF16)
    gw_s = sbt("gw_s", [128, NCH * 8], BF16)

    ABF = 78400
    AFS = 8500
    abf_t = sbt("abf", [128, ABF], BF16)
    af_t = sbt("af32", [128, AFS], F32)
    abf = Arena(abf_t, ABF, "abf")
    af32 = Arena(af_t, AFS, "af32")

    pb = [g.enter_context(nc.psum_tensor(f"pb{i}", [128, 512], F32))
          for i in range(7)]
    pbf = g.enter_context(nc.psum_tensor("pbf", [128, 1024], BF16))

    cPE = Ctr(g, nc, "cPE"); cVE = Ctr(g, nc, "cVE"); cSE = Ctr(g, nc, "cSE")
    cPO = Ctr(g, nc, "cPO")
    cSD = Ctr(g, nc, "cSD", dma=True); cGD = Ctr(g, nc, "cGD", dma=True)
    cSP = Ctr(g, nc, "cSP")
    ALLC = (cPE, cVE, cSE, cPO, cSD, cGD, cSP)
    BARC = {SP: cSD, PE: cPE, VE: cVE, SE: cSE, PO: cGD}
    COMPC = {SP: cSP, PE: cPE, VE: cVE, SE: cSE, PO: cPO}

    def snap():
        return [(c.h, c.n) for c in ALLC]

    prev = [None]

    def play(P):
        P.play(nc, prev[0])
        prev[0] = snap()

    def newplan():
        return Plan(BARC, COMPC)

    xin_r = xin.rearrange("(c p) b n -> p b c n", p=128)

    P = newplan()
    for nm, dst in (("ident_f32", ident_f), ("ident_bf", ident_b),
                    ("jaux", jaux_s), ("sel16", sel16_s), ("sel8", sel8_s),
                    ("usq", usq_s), ("A_log", acol_s), ("conv_w", cw_s),
                    ("conv_b", cb_s)):
        P.do(SP, lambda e, nm=nm, dst=dst: e.dma_start(out=dst[:], in_=wdr[nm]),
             inc=(cSD, 16))
    for nm, col in (("bq", BQ), ("bk", BK), ("bv", BV), ("bo", BO),
                    ("bfq", BFQG), ("bfg", BFQG + 8)):
        P.do(SP, lambda e, nm=nm, col=col: e.dma_start(
            out=bias_s[:, col:col + 8], in_=wdr[nm]), inc=(cSD, 16))
    sd0 = cSD.n
    P.do(PO, lambda e: e.dma_start(out=upz_s[:], in_=wdr["U_pz"]), inc=(cGD, 16))
    P.do(PO, lambda e: e.dma_start(
        out=bass.AP(wx_s, 0, [[16 * 80, 128], [80, 16], [1, 80]]),
        in_=wdr["W_xTp"].rearrange("(c p) n -> p c n", p=128)), inc=(cGD, 16))
    P.do(PO, lambda e: e.dma_start(
        out=bass.AP(gw_s, 0, [[NCH * 8, 128], [8, NCH], [1, 8]]),
        in_=wdr["gate_w"].rearrange("(c p) n -> p c n", p=128)), inc=(cGD, 16))
    P.do(VE, lambda e: e.memset(ones_f[:], 1.0))
    P.do(VE, lambda e: e.memset(eps_s[:], 1e-5))
    P.do(VE, lambda e: e.memset(ones_b[:], 1.0))
    P.do(VE, lambda e: e.tensor_scalar_mul(
        bias_s[:, BQ:BQ + 8], bias_s[:, BQ:BQ + 8], 0.125), waits=[(cSD, sd0)])
    P.do(VE, lambda e: e.tensor_add(
        bias_s[:, BFQG:BFQG + 8], bias_s[:, BFQG:BFQG + 8],
        bias_s[:, BFQG + 8:BFQG + 16]), drain=True)
    P.do(VE, lambda e: e.tensor_copy(bv16_s[:], bias_s[:, BV:BV + 8]), drain=True)
    se_a = P.do(SE, lambda e: e.activation(acol_s[:], acol_s[:], AF.Exp),
                inc=cSE, waits=[(cSD, sd0)])
    P.do(VE, lambda e: e.tensor_scalar_mul(acol_s[:], acol_s[:], -1.0),
         inc=cVE, waits=[(cSE, se_a)], drain=True)
    play(P)

    def phase_attn(b):
        if upto == "p0":
            return
        P = newplan()
        xh = abf.alloc("xh", NCH * S)
        wsl = [abf.alloc(f"wsl{i}", NCH * 1024) for i in range(2)]
        mq = abf.alloc("mq", NCH * S)
        mk = abf.alloc("mk", NCH * S)
        sq8 = abf.alloc("sq8", NCH * TC)
        xc16 = abf.alloc("xc16", NCH * TC)
        nmrep = abf.alloc("nmrep", TC)
        rsrep = abf.alloc("rsrep", TC)
        ttmp = abf.alloc("ttmp", TC)
        glo16 = abf.alloc("glo16", 8)
        bst16 = abf.alloc("bst16", 1)
        bstrow = abf.alloc("bstrow", 16)
        vroll = [abf.alloc(f"vroll{i}", 1040) for i in range(2)]

        stg = af32.alloc("stg", NCH * TC)
        rows = af32.alloc("rows", 4 * TC)
        qb = af32.alloc("qb", 6)
        glo = af32.alloc("glo", 8)
        gvec = af32.alloc("gvec", 8)
        bstrep = af32.alloc("bstrep", 16)
        cmp_ = af32.alloc("cmp", 144 + 512)
        O_E, O_D, O_RP, O_U, O_RZ, O_TA, O_T2, O_CF = \
            0, 32, 64, 80, 96, 112, 128, 144
        rq = af32.alloc("rq", 4)

        def xhc(c, t0, t1):
            return xh[:, c * S + t0:c * S + t1]

        po_cast, se_sq, pe_ln, ve_var, se_rstd, pe_rep, ve_xln, sd_x = \
            [], [], [], [], [], [], [], []
        for tc in range(NTC_B):
            w = [(cPO, po_cast[tc - 1])] if tc >= 1 else []
            sd_x.append(P.do(SP, lambda e, tc=tc: e.dma_start(
                out=stg.ap([TC, NCH], [1, TC]),
                in_=xin_r[:, b, :, tc * TC:(tc + 1) * TC]),
                inc=(cSD, 16), waits=w))
            w = [(cSD, sd_x[tc])]
            if tc >= 1:
                w += [(cVE, ve_xln[tc - 1]), (cSE, se_sq[tc - 1]),
                      (cPE, pe_ln[tc - 1])]
            po_cast.append(P.do(PO, lambda e: e.tensor_copy(xc16[:, :], stg[:, :]),
                                inc=cPO, waits=w))
            w = [(cPO, po_cast[tc])]
            if tc >= 1:
                w.append((cPE, pe_ln[tc - 1]))
            tok = None
            for c in range(NCH):
                tok = P.do(SE, lambda e, c=c: e.activation(
                    sq8[:, c * TC:(c + 1) * TC], xc16[:, c * TC:(c + 1) * TC],
                    AF.Square), inc=cSE if c == NCH - 1 else None,
                    waits=w if c == 0 else ())
            se_sq.append(tok)
            w = [(cPO, po_cast[tc])]
            if tc >= 1:
                w.append((cVE, ve_var[tc - 1]))
            for c in range(NCH):
                P.do(PE, lambda e, c=c: e.matmul(
                    pb[0][0:1, :], ones_b[:, 0:1], xc16[:, c * TC:(c + 1) * TC],
                    start=(c == 0), stop=(c == NCH - 1)), waits=w if c == 0 else ())
            for c in range(NCH):
                tok = P.do(PE, lambda e, c=c: e.matmul(
                    pb[1][0:1, :], ones_b[:, 0:1], sq8[:, c * TC:(c + 1) * TC],
                    start=(c == 0), stop=(c == NCH - 1)),
                    inc=cPE if c == NCH - 1 else None,
                    waits=[(cSE, se_sq[tc])] if c == 0 else ())
            pe_ln.append(tok)
            w = [(cPE, pe_ln[tc])]
            if tc >= 1:
                w.append((cPE, pe_rep[tc - 1]))
            P.do(VE, lambda e: e.tensor_scalar_mul(
                rows[0:1, 0:TC], pb[0][0:1, :], -1.0 / DM), waits=w)
            P.do(VE, lambda e: e.tensor_scalar_mul(
                rows[0:1, TC:2 * TC], pb[1][0:1, :], 1.0 / DM))
            P.do(VE, lambda e: e.tensor_mul(
                rows[0:1, 2 * TC:3 * TC], rows[0:1, 0:TC], rows[0:1, 0:TC]),
                drain=True)
            ve_var.append(P.do(VE, lambda e: e.tensor_sub(
                rows[0:1, 2 * TC:3 * TC], rows[0:1, TC:2 * TC],
                rows[0:1, 2 * TC:3 * TC]), inc=cVE, drain=True))
            P.do(SE, lambda e: e.activation(
                rows[0:1, 2 * TC:3 * TC], rows[0:1, 2 * TC:3 * TC], AF.Ln,
                bias=eps_s[0:1, 0:1]), waits=[(cVE, ve_var[tc])])
            se_rstd.append(P.do(SE, lambda e: e.activation(
                rows[0:1, 3 * TC:4 * TC], rows[0:1, 2 * TC:3 * TC], AF.Exp,
                scale=-0.5), inc=cSE, drain=True))
            wrep = [(cSE, se_rstd[tc])]
            if tc >= 1:
                wrep.append((cVE, ve_xln[tc - 1]))
            P.do(PE, lambda e: e.matmul(
                pb[2][:], ones_f[0:1, :], rows[0:1, 0:TC], start=True, stop=True),
                waits=wrep)
            pe_rep.append(P.do(PE, lambda e: e.matmul(
                pb[3][:], ones_f[0:1, :], rows[0:1, 3 * TC:4 * TC],
                start=True, stop=True), inc=cPE))
            P.do(VE, lambda e: e.tensor_copy(nmrep[:, :], pb[2][:]),
                 waits=[(cPE, pe_rep[tc])])
            P.do(VE, lambda e: e.tensor_copy(rsrep[:, :], pb[3][:]))
            for c in range(NCH):
                P.do(VE, lambda e, c=c: e.tensor_add(
                    ttmp[:, :], xc16[:, c * TC:(c + 1) * TC], nmrep[:, :]),
                    drain=True)
                tok = P.do(VE, lambda e, c=c, tc=tc: e.tensor_mul(
                    xhc(c, tc * TC, (tc + 1) * TC), ttmp[:, :], rsrep[:, :]),
                    inc=cVE if c == NCH - 1 else None, drain=True)
            ve_xln.append(tok)

        if upto == "pa1":
            play(P)
            abf.used = [u for u in abf.used]
            abf.used.clear()
            af32.used.clear()
            return
        wnames = ["wq", "wk", "wv", "wfg", "wfq", "wo"]
        gd_w = {}
        ve_wsc = {}

        def load_w(wn, wait_pe=None, wait_ve=None):
            slot = wsl[wnames.index(wn) % 2]
            w = []
            if wait_pe:
                w.append((cPE, wait_pe))
            if wait_ve:
                w.append((cVE, wait_ve))
            gd_w[wn] = P.do(PO, lambda e, wn=wn, slot=slot: e.dma_start(
                out=slot.ap([1024, NCH], [1, 1024]),
                in_=wdr[wn].rearrange("(c p) n -> p c n", p=128)),
                inc=(cGD, 16), waits=w)

        def scale_w(wn, sc):
            slot = wsl[wnames.index(wn) % 2]
            ve_wsc[wn] = P.do(VE, lambda e, slot=slot, sc=sc: e.tensor_scalar_mul(
                slot[:, :], slot[:, :], sc), inc=cVE, waits=[(cGD, gd_w[wn])])

        def proj_pass(wn, dst, bias_col, bnd_col, prev_bnd=None):
            slot = wsl[wnames.index(wn) % 2]
            se_ev = []
            pe_g = []
            pe_usq = []
            ve_bnd = []
            for tc in range(NTC_B):
                sq_toks = []
                for fc in range(NCH):
                    gi = tc * NCH + fc
                    bank = pb[gi % 2]
                    w = []
                    if gi == 0:
                        w = [(cGD, gd_w[wn]), (cVE, ve_xln[NTC_B - 1])]
                        if wn in ve_wsc:
                            w.append((cVE, ve_wsc[wn]))
                    if gi >= 2:
                        w.append((cSE, se_ev[gi - 2]))
                    for kc in range(NCH):
                        tok = P.do(PE, lambda e, kc=kc, fc=fc, tc=tc, bank=bank,
                                   slot=slot: e.matmul(
                            bank[:],
                            slot[:, kc * 1024 + fc * 128:kc * 1024 + fc * 128 + 128],
                            xhc(kc, tc * TC, (tc + 1) * TC),
                            start=(kc == 0), stop=(kc == NCH - 1)),
                            inc=cPE if kc == NCH - 1 else None,
                            waits=w if kc == 0 else ())
                    pe_g.append(tok)
                    w2 = [(cPE, pe_g[-1])]
                    if tc >= 1 and fc == 0:
                        w2.append((cPE, pe_usq[tc - 1]))
                    if tc == 0 and fc == 0:
                        w2.append((cPE, pe_ln[NTC_B - 1]))
                    se_ev.append(P.do(SE, lambda e, bank=bank, fc=fc, tc=tc:
                                      e.activation(
                        dst[:, fc * S + tc * TC:fc * S + (tc + 1) * TC], bank[:],
                        AF.Identity, bias=bias_s[:, bias_col + fc:bias_col + fc + 1]),
                        inc=cSE, waits=w2))
                    sq_toks.append(P.do(SE, lambda e, fc=fc, tc=tc: e.activation(
                        sq8[:, fc * TC:(fc + 1) * TC],
                        dst[:, fc * S + tc * TC:fc * S + (tc + 1) * TC],
                        AF.Square), inc=cSE, drain=True))
                for fc in range(NCH):
                    w3 = [(cSE, sq_toks[fc])]
                    if fc == 0 and tc >= 1:
                        w3.append((cVE, ve_bnd[tc - 1]))
                    if fc == 0 and tc == 0 and wn == "wq":
                        w3.append((cVE, ve_xln[NTC_B - 1]))
                    if fc == 0 and tc == 0 and prev_bnd:
                        w3.append((cVE, prev_bnd))
                    tok = P.do(PE, lambda e, fc=fc: e.matmul(
                        pb[2][0:16, :], usq_s[:, fc * 16:(fc + 1) * 16],
                        sq8[:, fc * TC:(fc + 1) * TC],
                        start=(fc == 0), stop=(fc == NCH - 1)),
                        inc=cPE if fc == NCH - 1 else None, waits=w3)
                pe_usq.append(tok)
                P.do(VE, lambda e: e.tensor_reduce(
                    rq[0:16, 0:1], pb[2][0:16, :], AX.X, ALU.max),
                    waits=[(cPE, pe_usq[tc])])
                if tc == 0:
                    t = P.do(VE, lambda e: e.tensor_copy(
                        qb[0:16, bnd_col], rq[0:16, 0:1]), inc=cVE, drain=True)
                else:
                    t = P.do(VE, lambda e: e.tensor_max(
                        qb[0:16, bnd_col], qb[0:16, bnd_col], rq[0:16, 0:1]),
                        inc=cVE, drain=True)
                ve_bnd.append(t)
            return pe_g[-1], ve_bnd[-1]

        load_w("wq")
        scale_w("wq", 0.125)
        load_w("wk")
        pe_q_done, ve_qb = proj_pass("wq", mq, BQ, 0)
        load_w("wv", wait_pe=pe_q_done)
        pe_k_done, ve_kb = proj_pass("wk", mk, BK, 1, prev_bnd=ve_qb)
        load_w("wfg", wait_pe=pe_k_done)
        scale_w("wfg", 8.0)

        if upto == "proj":
            play(P)
            abf.used = [u for u in abf.used]
            abf.used.clear()
            af32.used.clear()
            return
        slotv = wsl[wnames.index("wv") % 2]
        sd_v = []
        ve_vev = []
        gi = 0
        for tc in range(NTC_B):
            for tt in range(4):
                tg = tc * 4 + tt
                vr = vroll[tg % 2]
                if tg < 2:
                    P.do(VE, lambda e, vr=vr: e.memset(
                        vr.ap([65, 16], off=64), 1.0))
                for fcol in range(2):
                    bank = pb[gi % 2]
                    w = []
                    if gi == 0:
                        w = [(cGD, gd_w["wv"]), (cSE, cSE.n)]
                    if gi >= 2:
                        w.append((cVE, ve_vev[gi - 2]))
                    for kc in range(NCH):
                        tok = P.do(PE, lambda e, kc=kc, tc=tc, tt=tt, fcol=fcol,
                                   bank=bank: e.matmul(
                            bank[:],
                            xhc(kc, tc * TC + tt * 128, tc * TC + tt * 128 + 128),
                            slotv[:, kc * 1024 + fcol * 512:kc * 1024 + fcol * 512 + 512],
                            start=(kc == 0), stop=(kc == NCH - 1)),
                            inc=cPE if kc == NCH - 1 else None,
                            waits=w if kc == 0 else ())
                    w2 = [(cPE, tok)]
                    if tg >= 2 and fcol == 0:
                        w2.append((cSD, sd_v[tg - 2]))
                    ve_vev.append(P.do(VE, lambda e, bank=bank, vr=vr, fcol=fcol:
                                       e.tensor_copy(
                        vr.ap([65, 8], [1, 64], off=fcol * 8 * 65), bank[:]),
                        inc=cVE, waits=w2))
                    gi += 1
                sd_v.append(P.do(SP, lambda e, vr=vr, tg=tg: e.dma_start(
                    out=v_dram[b, :, tg * 1040:(tg + 1) * 1040], in_=vr[:, :]),
                    inc=(cSD, 16), waits=[(cVE, ve_vev[-1])]))
        pe_v_done = tok

        load_w("wfq", wait_pe=pe_v_done)
        scale_w("wfq", 8.0)

        if upto == "wfq":
            play(P)
            abf.used.clear()
            af32.used.clear()
            return
        for c in range(NCH):
            P.do(VE, lambda e, c=c: e.tensor_reduce(
                glo[:, c], mq[:, c * S:(c + 1) * S], AX.X, ALU.add),
                waits=[(cSE, cSE.n)] if c == 0 else ())
        P.do(VE, lambda e: e.tensor_scalar_mul(glo[:, 0:8], glo[:, 0:8], 1.0 / S),
             drain=True)
        ve_glo = P.do(VE, lambda e: e.tensor_copy(glo16[:, :], glo[:, 0:8]),
                      inc=cVE, drain=True)
        slotfg = wsl[wnames.index("wfg") % 2]
        first = [True]
        for fc in range(NCH):
            for kc in range(NCH):
                w = []
                if first[0]:
                    w = [(cVE, ve_glo), (cVE, ve_wsc["wfg"])]
                st_ = first[0]
                tok = P.do(PE, lambda e, fc=fc, kc=kc, st_=st_: e.matmul(
                    pb[3][:, fc:fc + 1],
                    slotfg[:, kc * 1024 + fc * 128:kc * 1024 + fc * 128 + 128],
                    glo16[:, kc], start=st_,
                    stop=(fc == NCH - 1 and kc == NCH - 1)),
                    inc=cPE if (fc == NCH - 1 and kc == NCH - 1) else None,
                    waits=w)
                first[0] = False
        pe_gv = tok
        ve_gv = P.do(VE, lambda e: e.tensor_add(
            gvec[:, 0:8], pb[3][:, 0:8], bias_s[:, BFQG:BFQG + 8]),
            inc=cVE, waits=[(cPE, pe_gv)])

        if upto == "gvec":
            play(P)
            abf.used.clear()
            af32.used.clear()
            return
        P.do(SE, lambda e: e.activation(qb[0:16, 2], qb[0:16, 0], AF.Ln),
             waits=[(cVE, ve_qb), (cVE, ve_kb)])
        P.do(SE, lambda e: e.activation(qb[0:16, 2], qb[0:16, 2], AF.Exp,
                                        scale=0.5), drain=True)
        P.do(SE, lambda e: e.activation(qb[0:16, 3], qb[0:16, 1], AF.Ln))
        se_bn = P.do(SE, lambda e: e.activation(qb[0:16, 3], qb[0:16, 3], AF.Exp,
                                                scale=0.5), inc=cSE, drain=True)
        P.do(VE, lambda e: e.tensor_mul(qb[0:16, 4], qb[0:16, 2], qb[0:16, 3]),
             waits=[(cSE, se_bn)], drain=True)
        ve_bst = P.do(VE, lambda e: e.tensor_copy(bst16[0:16, 0:1], qb[0:16, 4]),
                      inc=cVE, drain=True)
        pe_bt = P.do(PE, lambda e: e.transpose(
            pbf[0:1, 0:16], bst16[0:16, 0:1], ident_b[0:16, 0:16]),
            inc=cPE, waits=[(cVE, ve_bst)])
        ve_btr = P.do(VE, lambda e: e.tensor_copy(
            bstrow[0:1, 0:16], pbf[0:1, 0:16]), inc=cVE, waits=[(cPE, pe_bt)])
        pe_br = P.do(PE, lambda e: e.matmul(
            pb[4][:, 0:16], ones_b[0:1, :], bstrow[0:1, 0:16],
            start=True, stop=True), inc=cPE, waits=[(cVE, ve_btr)])
        ve_brep = P.do(VE, lambda e: e.tensor_copy(
            bstrep[:, 0:16], pb[4][:, 0:16]), inc=cVE, waits=[(cPE, pe_br)])

        if upto == "wv":
            play(P)
            abf.used = [u for u in abf.used]
            abf.used.clear()
            af32.used.clear()
            return
        if upto == "bst":
            play(P)
            abf.used.clear()
            af32.used.clear()
            return
        abf.free(xh, sq8, xc16, nmrep, rsrep, ttmp)
        c16 = abf.alloc_high("c16", NCH * TC)
        abc = abf.alloc_high("abc", 4 * S)
        expb = [abf.alloc_high(f"expb{i}", TC) for i in range(4)]
        ctxp = [abf.alloc_high(f"ctxp{i}", 128) for i in range(4)]
        ctxall = abf.alloc_high("ctxall", NCH * TC)
        otile = abf.alloc_high("otile", NCH * TC)

        P.do(VE, lambda e: e.memset(cmp_[:, O_CF:O_CF + 512], 0.0))
        slotfq = wsl[wnames.index("wfq") % 2]
        se_c = []
        pe_pz = []
        ve_abc = []
        pe_tr = []
        for tc in range(NTC_B):
            for fc in range(NCH):
                gi2 = tc * NCH + fc
                bank = pb[gi2 % 2]
                w = []
                if gi2 == 0:
                    w = [(cVE, ve_gv), (cVE, ve_wsc["wfq"])]
                if gi2 >= 2:
                    w.append((cSE, se_c[gi2 - 2]))
                for kc in range(NCH):
                    tok = P.do(PE, lambda e, kc=kc, fc=fc, tc=tc, bank=bank:
                               e.matmul(
                        bank[:],
                        slotfq[:, kc * 1024 + fc * 128:kc * 1024 + fc * 128 + 128],
                        mq[:, kc * S + tc * TC:kc * S + (tc + 1) * TC],
                        start=(kc == 0), stop=(kc == NCH - 1)),
                        inc=cPE if kc == NCH - 1 else None,
                        waits=w if kc == 0 else ())
                w2 = [(cPE, tok)]
                if tc >= 1 and fc < 2 and pe_pz:
                    w2.append((cPE, pe_pz[4 * (tc - 1) + 3]))
                se_c.append(P.do(SE, lambda e, bank=bank, fc=fc: e.activation(
                    c16[:, fc * TC:(fc + 1) * TC], bank[:], AF.Tanh,
                    bias=gvec[:, fc]), inc=cSE, waits=w2))
            for tt in range(4 if upto not in ("cmm",) else 0):
                w = [(cSE, se_c[tc * NCH + NCH - 1])]
                if ve_abc:
                    w.append((cVE, ve_abc[-1]))
                else:
                    w.append((cVE, ve_brep))
                for fc in range(NCH):
                    tok = P.do(PE, lambda e, fc=fc, tc=tc, tt=tt: e.matmul(
                        pb[4][:, 0:32],
                        c16[:, fc * TC + tt * 128:fc * TC + tt * 128 + 128],
                        upz_s[:, fc * 32:(fc + 1) * 32],
                        start=(fc == 0), stop=(fc == NCH - 1)),
                        inc=cPE if fc == NCH - 1 else None,
                        waits=w if fc == 0 else ())
                pe_pz.append(tok)
                se_e = P.do(SE, lambda e: e.activation(
                    cmp_[:, O_E:O_E + 32], pb[4][:, 0:32], AF.Exp, scale=-1.0),
                    inc=cSE, waits=[(cPE, pe_pz[-1])])
                P.do(VE, lambda e: e.tensor_scalar_add(
                    cmp_[:, O_D:O_D + 32], cmp_[:, O_E:O_E + 32], 1.0),
                    waits=[(cSE, se_e)])
                P.do(VE, lambda e: e.reciprocal(
                    cmp_[:, O_RP:O_RP + 16], cmp_[:, O_D:O_D + 16]), drain=True)
                P.do(VE, lambda e: e.tensor_scalar(
                    cmp_[:, O_U:O_U + 16], cmp_[:, O_RP:O_RP + 16],
                    2048.0, -1024.0, ALU.mult, ALU.add), drain=True)
                P.do(VE, lambda e: e.tensor_scalar_mul(
                    cmp_[:, O_RZ:O_RZ + 16], cmp_[:, O_D + 16:O_D + 32],
                    1.0 / 2048.0), drain=True)
                P.do(VE, lambda e: e.tensor_mul(
                    cmp_[:, O_TA:O_TA + 16], cmp_[:, O_RZ:O_RZ + 16],
                    cmp_[:, O_RZ:O_RZ + 16]), drain=True)

                if upto == "chain1":
                    continue
                def dst_ap(compoff):
                    return bass.AP(af_t, cmp_.off + O_CF + compoff,
                                   [[AFS, 128], [128, 4], [32, 4]])
                P.do(VE, lambda e: e.tensor_scalar_mul(
                    dst_ap(0), cmp_[:, O_TA:O_TA + 16], -2.0), drain=True)
                P.do(VE, lambda e: e.tensor_scalar_mul(
                    dst_ap(1), cmp_[:, O_TA:O_TA + 16], -2.0))
                P.do(VE, lambda e: e.tensor_scalar_mul(
                    cmp_[:, O_TA:O_TA + 16], cmp_[:, O_TA:O_TA + 16], -2.0),
                    drain=True)
                P.do(VE, lambda e: e.tensor_mul(
                    cmp_[:, O_T2:O_T2 + 16], cmp_[:, O_TA:O_TA + 16],
                    cmp_[:, O_U:O_U + 16]), drain=True)
                P.do(VE, lambda e: e.tensor_scalar_mul(
                    dst_ap(2), cmp_[:, O_T2:O_T2 + 16], -2.0), drain=True)
                P.do(VE, lambda e: e.tensor_scalar_mul(
                    dst_ap(3), cmp_[:, O_T2:O_T2 + 16], -2.0))
                P.do(VE, lambda e: e.tensor_mul(
                    cmp_[:, O_T2:O_T2 + 16], cmp_[:, O_T2:O_T2 + 16],
                    cmp_[:, O_U:O_U + 16]), drain=True)
                ve_cf = P.do(VE, lambda e: e.tensor_sub(
                    dst_ap(4), cmp_[:, O_T2:O_T2 + 16], bstrep[:, 0:16]),
                    inc=cVE, drain=True)
                for gg in range(4 if upto != "notr" else 0):
                    gti = len(pe_tr)
                    bank = pb[5 + gg % 2]
                    w = [(cVE, ve_cf)] if gg == 0 else []
                    if gti >= 2:
                        w.append((cVE, ve_abc[gti - 2]))
                    pe_tr.append(P.do(PE, lambda e, gg=gg, bank=bank: e.transpose(
                        bank[:, 0:128],
                        cmp_[:, O_CF + gg * 128:O_CF + (gg + 1) * 128],
                        ident_f[:]), inc=cPE, waits=w))
                    ve_abc.append(P.do(VE, lambda e, gg=gg, tc=tc, tt=tt,
                                       bank=bank: e.tensor_copy(
                        abc[:, gg * S + tc * TC + tt * 128:
                            gg * S + tc * TC + tt * 128 + 128],
                        bank[:, 0:128]), inc=cVE, waits=[(cPE, pe_tr[-1])]))

        if upto in ("cmm", "cfull", "notr", "pzonly", "chain1"):
            play(P)
            abf.used.clear()
            af32.used.clear()
            return
        load_w("wo", wait_pe=pe_gv)
        slotwo = wsl[wnames.index("wo") % 2]
        first = [True]
        for fc in range(NCH):
            for kc in range(NCH):
                w = []
                if first[0]:
                    w = [(cGD, gd_w["wo"]), (cVE, ve_brep)]
                    if ve_abc:
                        w.append((cVE, ve_abc[-1]))
                st_ = first[0]
                tok = P.do(PE, lambda e, fc=fc, kc=kc, st_=st_: e.matmul(
                    pb[6][:, fc:fc + 1],
                    slotwo[:, kc * 1024 + fc * 128:kc * 1024 + fc * 128 + 128],
                    bv16_s[:, kc:kc + 1], start=st_,
                    stop=(fc == NCH - 1 and kc == NCH - 1)),
                    inc=cPE if (fc == NCH - 1 and kc == NCH - 1) else None,
                    waits=w)
                first[0] = False
        ve_bobv = P.do(VE, lambda e: e.tensor_add(
            bias_s[:, BOBV:BOBV + 8], pb[6][:, 0:8], bias_s[:, BO:BO + 8]),
            inc=cVE, waits=[(cPE, tok)])

        if upto == "cpass":
            play(P)
            abf.used = [u for u in abf.used]
            abf.used.clear()
            af32.used.clear()
            return
        abf.free(wsl[0], c16, vroll[0], vroll[1], glo16, bst16, bstrow)
        vpb = [abf.alloc(f"vpb{i}", 1040) for i in range(16)]
        sd_vr = None
        for i in range(16):
            w0 = []
            if i == 0:
                w0 = [(cSD, sd_v[-1]), (cPE, pe_pz[-1]), (cVE, ve_abc[-1])]
            sd_vr = P.do(SP, lambda e, i=i: e.dma_start(
                out=vpb[i][:, :], in_=v_dram[b, :, i * 1040:(i + 1) * 1040]),
                inc=(cSD, 16), waits=w0)

        ve_rec = []
        sd_rec = []
        sd_osp = []
        se_exp = {}
        pe_pv = {}
        ve_ctx = {}
        pe_wo_last = [None]
        for qc in range(NTC_B):
            w = [(cPO, po_cast[-1])]
            if qc >= 1:
                w.append((cVE, ve_rec[-1]))
            sd_rec.append(P.do(SP, lambda e, qc=qc: e.dma_start(
                out=stg.ap([TC, NCH], [1, TC]),
                in_=xin_r[:, b, :, qc * TC:(qc + 1) * TC]),
                inc=(cSD, 16), waits=w))
            for h in range(H):
                hc, hr = h // 2, (h % 2) * 64
                slot4 = h % 4
                fg = h // 4
                cbank = pb[2 + (h % 2)]
                for kt in range(16):
                    sbank = pb[kt % 2]
                    w = []
                    if h == 0 and kt == 0:
                        if qc == 0:
                            w += [(cVE, ve_bobv), (cVE, ve_abc[-1]),
                                  (cSD, sd_vr)]
                        else:
                            w.append((cVE, ve_rec[-1]))
                    if kt >= 2:
                        pk = (qc, h, kt - 2)
                    elif h >= 1:
                        pk = (qc, h - 1, 14 + kt)
                    else:
                        pk = (qc - 1, H - 1, 14 + kt)
                    if pk in se_exp:
                        w.append((cSE, se_exp[pk]))
                    P.do(PE, lambda e, hc=hc, hr=hr, kt=kt, qc=qc, sbank=sbank:
                         e.matmul(
                        sbank[:],
                        mk[hr:hr + 64, hc * S + kt * 128:hc * S + kt * 128 + 128],
                        mq[hr:hr + 64, hc * S + qc * TC:hc * S + (qc + 1) * TC],
                        start=True, stop=False), waits=w)
                    tok = P.do(PE, lambda e, slot4=slot4, fg=fg, kt=kt, qc=qc,
                               sbank=sbank: e.matmul(
                        sbank[:],
                        jaux_s[slot4 * 32:slot4 * 32 + 32, kt * 128:kt * 128 + 128],
                        abc[slot4 * 32:slot4 * 32 + 32,
                            fg * S + qc * TC:fg * S + (qc + 1) * TC],
                        start=False, stop=True,
                        tile_position=(slot4 * 32, 0)), inc=cPE)
                    w2 = [(cPE, tok)]
                    if kt >= 4:
                        pv = (qc, h, kt - 4)
                    elif h >= 1:
                        pv = (qc, h - 1, 12 + kt)
                    else:
                        pv = (qc - 1, H - 1, 12 + kt)
                    if pv in pe_pv:
                        w2.append((cPE, pe_pv[pv]))
                    se_exp[(qc, h, kt)] = P.do(
                        SE, lambda e, kt=kt, sbank=sbank: e.activation(
                            expb[kt % 4][:, :], sbank[:], AF.Exp),
                        inc=cSE, waits=w2)
                    w3 = [(cSE, se_exp[(qc, h, kt)])]
                    if kt == 0:
                        ck = (qc, h - 2) if h >= 2 else (qc - 1, H - 2 + h)
                        if ck in ve_ctx:
                            w3.append((cVE, ve_ctx[ck]))
                    for qt in range(4):
                        tok = P.do(PE, lambda e, kt=kt, qt=qt, h=h, cbank=cbank:
                                   e.matmul(
                            cbank[:, qt * 65:qt * 65 + 65],
                            expb[kt % 4][:, qt * 128:qt * 128 + 128],
                            vpb[kt][:, h * 65:(h + 1) * 65],
                            start=(kt == 0 and qt == 0),
                            stop=(kt == 15 and qt == 3)),
                            inc=cPE if qt == 3 else None,
                            waits=w3 if qt == 0 else ())
                    pe_pv[(qc, h, kt)] = tok
                ve_r = P.do(VE, lambda e, cbank=cbank: e.reciprocal(
                    rq[:, 0:4], bass.AP(cbank, 64, [[512, 128], [65, 4]])),
                    inc=cVE, waits=[(cPE, pe_pv[(qc, h, 15)])], drain=True)
                for qt in range(4):
                    w4 = []
                    if qt == 0:
                        w4.append((cVE, ve_r))
                        if h % 2 == 0:
                            ck = (qc, h - 1) if h >= 1 else (qc - 1, H - 1)
                            if ck in ve_ctx:
                                w4.append((cVE, ve_ctx[ck]))
                    tok = P.do(VE, lambda e, qt=qt, h=h, cbank=cbank:
                               e.tensor_single_scalar(
                        ctxp[qt][:, (h % 2) * 64:(h % 2) * 64 + 64],
                        bass.AP(cbank, qt * 65, [[512, 128], [1, 64]]),
                        rq[:, qt], ALU.mult),
                        inc=cVE if qt == 3 else None,
                        waits=w4, drain=(qt == 0))
                ve_ctx[(qc, h)] = tok
                if h % 2 == 1:
                    evtok = None
                    for qt in range(4):
                        w4 = [(cVE, ve_ctx[(qc, h)])] if qt == 0 else []
                        if qt > 0:
                            w4.append((cVE, evtok))
                        if qt == 0 and h == 1 and pe_wo_last[0] is not None:
                            w4.append((cPE, pe_wo_last[0]))
                        ptok = P.do(PE, lambda e, qt=qt: e.transpose(
                            pbf[:, qt * 128:(qt + 1) * 128],
                            ctxp[qt][:, 0:128], ident_b[:]),
                            inc=cPE, waits=w4)
                        evtok = P.do(VE, lambda e, qt=qt, hc=hc: e.tensor_copy(
                            ctxall[:, hc * TC + qt * 128:hc * TC + (qt + 1) * 128],
                            pbf[:, qt * 128:(qt + 1) * 128]),
                            inc=cVE, waits=[(cPE, ptok)])
                    ve_ctx[(qc, h)] = evtok
            pe_wo = []
            for fc in range(NCH):
                bank = pb[4 + fc % 2]
                w = [(cVE, ve_ctx[(qc, H - 1)]), (cSD, sd_rec[qc])]
                if fc == 0 and qc == 0:
                    w.append((cSE, se_c[-1]))
                if fc >= 2:
                    w.append((cVE, ve_rec[-2]))
                if fc < 2 and qc >= 1:
                    w.append((cVE, ve_rec[-2]))
                for kc in range(NCH):
                    tok = P.do(PE, lambda e, kc=kc, fc=fc, bank=bank: e.matmul(
                        bank[:],
                        slotwo[:, kc * 1024 + fc * 128:kc * 1024 + fc * 128 + 128],
                        ctxall[:, kc * TC:(kc + 1) * TC],
                        start=(kc == 0), stop=(kc == NCH - 1)),
                        inc=cPE if kc == NCH - 1 else None,
                        waits=w if kc == 0 else ())
                pe_wo.append(tok)
                wv5 = [(cPE, pe_wo[fc])]
                if qc >= 1 and fc == 0:
                    wv5.append((cSD, sd_osp[-1]))
                ve_rec.append(P.do(VE, lambda e, fc=fc, bank=bank:
                                   e.scalar_tensor_tensor(
                    otile[:, fc * TC:(fc + 1) * TC], bank[:],
                    bias_s[:, BOBV + fc:BOBV + fc + 1], stg[:, fc * TC:(fc + 1) * TC],
                    ALU.add, ALU.add), inc=cVE, waits=wv5))
            pe_wo_last[0] = pe_wo[-1]
            sd_osp.append(P.do(SP, lambda e, qc=qc: e.dma_start(
                out=bass.AP(xsp.tensor, b * S + qc * TC,
                            [[NCH * T, 128], [T, NCH], [1, TC]]),
                in_=otile.ap([TC, NCH], [1, TC])),
                inc=(cSD, 16), waits=[(cVE, ve_rec[-1])]))

        play(P)
        abf.free(mq, mk, wsl[1], abc, expb[0], expb[1], expb[2],
                 expb[3], ctxp[0], ctxp[1], ctxp[2], ctxp[3], ctxall, otile,
                 *vpb)
        af32.free(stg, rows, qb, glo, gvec, bstrep, cmp_, rq)

    phase_attn(0)
    if upto not in ("pa1", "proj", "wv", "wfq", "gvec", "bst", "cmm", "cfull",
                    "notr", "pzonly", "chain1", "p0"):
        phase_attn(1)

    def xsp_ap(tcg, w=TC):
        bq_, tq = divmod(tcg, NTC_B)
        return bass.AP(xsp.tensor, bq_ * S + tq * TC,
                       [[NCH * T, 128], [T, NCH], [1, w]])

    def oac_ap(tcg, w=TC):
        bq_, tq = divmod(tcg, NTC_B)
        return bass.AP(oac.tensor, bq_ * S + tq * TC,
                       [[NCH * T, 128], [T, NCH], [1, w]])

    def phase_ffn(inv):
        P = newplan()
        xw = abf.alloc("xw", NCH * T)
        xt16 = [abf.alloc("xt16A", NCH * TC)]
        xt16.append(xt16[0])
        sq = abf.alloc("sqF", NCH * TC)
        nmrep = abf.alloc("nmrepF", TC)
        rsrep = abf.alloc("rsrepF", TC)
        ttmp = abf.alloc("ttmpF", TC)
        wg = [[abf.alloc(f"w{gi}_{j}", 8 * 512 if j < 2 else 4 * 1024)
               for j in range(3)] for gi in range(2)]
        sil = abf.alloc("sil", 4 * TC)
        inter = abf.alloc("inter", 4 * TC)
        w_cm = abf.alloc("w_cm", T)
        wt16 = abf.alloc("wt16", 32)
        w_rep = [abf.alloc(f"wrep{i}", TC) for i in range(2)]
        outsb = af32.alloc("outsb", NCH * TC)
        rows = af32.alloc("rowsF", 4 * TC)
        gt = af32.alloc("gt", 64)

        def xwc(c, tcg):
            return xw[:, c * T + tcg * TC:c * T + tcg * TC + TC]

        sd_x = []
        se_sq = []
        pe_ln = []
        ve_var = []
        se_rstd = []
        pe_rep = []
        ve_xln = []
        ve_gt = []
        ve_wcm = []
        for tcg in range(NTC):
            roll = xt16[0]
            w = []
            if tcg >= 1:
                w += [(cVE, ve_xln[tcg - 1]), (cSE, se_sq[tcg - 1]),
                      (cPE, pe_ln[tcg - 1])]
            sd_x.append(P.do(SP, lambda e, tcg=tcg, roll=roll: e.dma_start(
                out=roll.ap([TC, NCH], [1, TC]), in_=xsp_ap(tcg)),
                inc=(cSD, 16), waits=w))
            w = [(cSD, sd_x[tcg])]
            if tcg >= 1:
                w.append((cPE, pe_ln[tcg - 1]))
            tok = None
            for c in range(NCH):
                tok = P.do(SE, lambda e, c=c, roll=roll: e.activation(
                    sq[:, c * TC:(c + 1) * TC], roll[:, c * TC:(c + 1) * TC],
                    AF.Square), inc=cSE if c == NCH - 1 else None,
                    waits=w if c == 0 else ())
            se_sq.append(tok)
            w = [(cSD, sd_x[tcg])]
            if tcg >= 1:
                w.append((cVE, ve_var[tcg - 1]))
            for c in range(NCH):
                P.do(PE, lambda e, c=c, roll=roll: e.matmul(
                    pb[0][0:1, :], ones_b[:, 0:1], roll[:, c * TC:(c + 1) * TC],
                    start=(c == 0), stop=(c == NCH - 1)), waits=w if c == 0 else ())
            for c in range(NCH):
                tok = P.do(PE, lambda e, c=c: e.matmul(
                    pb[1][0:1, :], ones_b[:, 0:1], sq[:, c * TC:(c + 1) * TC],
                    start=(c == 0), stop=(c == NCH - 1)),
                    inc=cPE if c == NCH - 1 else None,
                    waits=[(cSE, se_sq[tcg])] if c == 0 else ())
            pe_ln.append(tok)
            w = [(cPE, pe_ln[tcg])]
            P.do(VE, lambda e: e.tensor_scalar_mul(
                rows[0:1, 0:TC], pb[0][0:1, :], -1.0 / DM), waits=w)
            P.do(VE, lambda e: e.tensor_scalar_mul(
                rows[0:1, TC:2 * TC], pb[1][0:1, :], 1.0 / DM))
            P.do(VE, lambda e: e.tensor_mul(
                rows[0:1, 2 * TC:3 * TC], rows[0:1, 0:TC], rows[0:1, 0:TC]),
                drain=True)
            ve_var.append(P.do(VE, lambda e: e.tensor_sub(
                rows[0:1, 2 * TC:3 * TC], rows[0:1, TC:2 * TC],
                rows[0:1, 2 * TC:3 * TC]), inc=cVE, drain=True))
            P.do(SE, lambda e: e.activation(
                rows[0:1, 2 * TC:3 * TC], rows[0:1, 2 * TC:3 * TC], AF.Ln,
                bias=eps_s[0:1, 0:1]), waits=[(cVE, ve_var[tcg])])
            se_rstd.append(P.do(SE, lambda e: e.activation(
                rows[0:1, 3 * TC:4 * TC], rows[0:1, 2 * TC:3 * TC], AF.Exp,
                scale=-0.5), inc=cSE, drain=True))
            wrep_ = [(cSE, se_rstd[tcg])]
            if tcg >= 1:
                wrep_.append((cVE, ve_xln[tcg - 1]))
            P.do(PE, lambda e: e.matmul(
                pb[2][:], ones_f[0:1, :], rows[0:1, 0:TC], start=True, stop=True),
                waits=wrep_)
            pe_rep.append(P.do(PE, lambda e: e.matmul(
                pb[3][:], ones_f[0:1, :], rows[0:1, 3 * TC:4 * TC],
                start=True, stop=True), inc=cPE))
            P.do(VE, lambda e: e.tensor_copy(nmrep[:, :], pb[2][:]),
                 waits=[(cPE, pe_rep[tcg])])
            P.do(VE, lambda e: e.tensor_copy(rsrep[:, :], pb[3][:]))
            for c in range(NCH):
                P.do(VE, lambda e, c=c, roll=roll: e.tensor_add(
                    ttmp[:, :], roll[:, c * TC:(c + 1) * TC], nmrep[:, :]),
                    drain=True)
                tok = P.do(VE, lambda e, c=c, tcg=tcg: e.tensor_mul(
                    xwc(c, tcg), ttmp[:, :], rsrep[:, :]),
                    inc=cVE if c == NCH - 1 else None, drain=True)
            ve_xln.append(tok)
            for tt in range(4):
                w = [(cVE, ve_xln[tcg])]
                if ve_gt:
                    w.append((cVE, ve_gt[-1]))
                for kc in range(NCH):
                    tok = P.do(PE, lambda e, kc=kc, tcg=tcg, tt=tt: e.matmul(
                        pb[4][:, 0:8],
                        xw[:, kc * T + tcg * TC + tt * 128:
                           kc * T + tcg * TC + tt * 128 + 128],
                        gw_s[:, kc * 8:(kc + 1) * 8],
                        start=(kc == 0), stop=(kc == NCH - 1)),
                        inc=cPE if kc == NCH - 1 else None,
                        waits=w if kc == 0 else ())
                pe_g = tok
                P.do(VE, lambda e: e.tensor_reduce(
                    gt[:, 16], pb[4][:, 0:8], AX.X, ALU.max),
                    waits=[(cPE, pe_g)], drain=True)
                P.do(VE, lambda e: e.tensor_single_scalar(
                    gt[:, 0:8], pb[4][:, 0:8], gt[:, 16], ALU.is_equal),
                    drain=True)
                P.do(VE, lambda e: e.scalar_tensor_tensor(
                    gt[:, 8:16], gt[:, 0:8], -1e30, pb[4][:, 0:8],
                    ALU.mult, ALU.add), drain=True)
                P.do(VE, lambda e: e.tensor_reduce(
                    gt[:, 17], gt[:, 8:16], AX.X, ALU.max), drain=True)
                t_d21 = P.do(VE, lambda e: e.tensor_sub(
                    gt[:, 18], gt[:, 17], gt[:, 16]), inc=cVE, drain=True)
                t_e2 = P.do(SE, lambda e: e.activation(
                    gt[:, 19], gt[:, 18], AF.Exp), inc=cSE,
                    waits=[(cVE, t_d21)])
                P.do(VE, lambda e: e.tensor_scalar_add(
                    gt[:, 20], gt[:, 19], 1.0), waits=[(cSE, t_e2)], drain=True)
                P.do(VE, lambda e: e.reciprocal(gt[:, 21], gt[:, 20]),
                     drain=True)
                P.do(VE, lambda e: e.tensor_scalar(
                    gt[:, 22], gt[:, 21], -1.0, 1.0, ALU.mult, ALU.add),
                    drain=True)
                P.do(VE, lambda e: e.tensor_single_scalar(
                    gt[:, 24:32], gt[:, 8:16], gt[:, 17], ALU.is_equal),
                    drain=True)
                P.do(VE, lambda e: e.tensor_single_scalar(
                    gt[:, 32:40], gt[:, 24:32], gt[:, 22], ALU.mult),
                    drain=True)
                P.do(VE, lambda e: e.scalar_tensor_tensor(
                    gt[:, 40:48], gt[:, 0:8], gt[:, 21], gt[:, 32:40],
                    ALU.mult, ALU.add), drain=True)
                ve_w = P.do(VE, lambda e: e.tensor_copy(
                    wt16[:, 0:8], gt[:, 40:48]), inc=cVE, drain=True)
                w4 = [(cVE, ve_w)]
                if ve_wcm:
                    w4.append((cVE, ve_wcm[-1]))
                ptok = P.do(PE, lambda e: e.transpose(
                    pbf[0:8, 0:128], wt16[:, 0:8], ident_b[:]),
                    inc=cPE, waits=w4)
                ve_wcm.append(P.do(VE, lambda e, tcg=tcg, tt=tt: e.tensor_copy(
                    w_cm[0:8, tcg * TC + tt * 128:tcg * TC + tt * 128 + 128],
                    pbf[0:8, 0:128]), inc=cVE, waits=[(cPE, ptok)]))
                ve_gt.append(ve_wcm[-1])

        subs = [("s", 0, q) for q in range(4)] + \
               [("e", ee, q) for ee in range(NEXP) for q in range(4)]
        gd_sub = []
        pe_sub_last = []
        ve_h2 = {}
        se_sil = {}
        se_out = {}
        gd_acc = []
        ve_wr = {}
        for si, (kind, ee, qq) in enumerate(subs):
            grp = wg[si % 2]
            w = []
            if si >= 2:
                w.append((cPE, pe_sub_last[si - 2]))
            if si == 0:
                w.append((cVE, ve_xln[0]))
            toks = []
            for j, nm in enumerate(("1", "2", "3")):
                if kind == "s":
                    src = wdr["sw" + nm]
                else:
                    src = wdr["ew" + nm][ee]
                if j < 2:
                    ap_in = src.rearrange("(c p) n -> p c n", p=128)[
                        :, :, qq * 512:(qq + 1) * 512]
                    ap_out = grp[j].ap([512, NCH], [1, 512])
                else:
                    ap_in = src[qq * 512:(qq + 1) * 512].rearrange(
                        "(c p) n -> p c n", p=128)
                    ap_out = grp[j].ap([1024, 4], [1, 1024])
                toks.append(P.do(PO, lambda e, ap_in=ap_in, ap_out=ap_out:
                                 e.dma_start(out=ap_out, in_=ap_in),
                                 inc=(cGD, 16), waits=w if j == 0 else ()))
            gd_sub.append(toks[-1])
            for tcg in range(NTC):
                if kind == "e":
                    wr = w_rep[tcg % 2]
                    w5 = [(cVE, ve_wcm[-1])] if (si <= 4 and tcg == 0) else []
                    if (si, tcg) != (4, 0) and (si > 4 or tcg > 0):
                        prev = ve_wr.get((si, tcg - 2)) or \
                            ve_wr.get((si - 1, NTC - 2 + tcg))
                        if prev:
                            w5.append((cVE, prev))
                    ptok = P.do(PE, lambda e, ee=ee, tcg=tcg: e.matmul(
                        pb[4][:, 0:512], sel8_s[0:8, ee * 128:(ee + 1) * 128],
                        w_cm[0:8, tcg * TC:(tcg + 1) * TC],
                        start=True, stop=True), inc=cPE, waits=w5)
                    ve_wr[(si, tcg)] = P.do(VE, lambda e, wr=wr: e.tensor_copy(
                        wr[:, :], pb[4][:, 0:512]), inc=cVE,
                        waits=[(cPE, ptok)])
                for hc in range(4):
                    bank = pb[hc % 2]
                    w6 = []
                    if si == 0 and tcg == 0 and hc == 0:
                        w6.append((cVE, ve_xln[NTC - 1]))
                    if (si, tcg) != (0, 0) or hc >= 2:
                        key = (si, tcg, hc - 2) if hc >= 2 else \
                            ((si, tcg - 1, 2 + hc) if tcg >= 1 else
                             (si - 1, NTC - 1, 2 + hc))
                        if key in se_sil:
                            w6.append((cSE, se_sil[key]))
                    if si == 0 and tcg == 0 and hc == 0:
                        w6.append((cGD, gd_sub[0]))
                    elif hc == 0 and tcg == 0:
                        w6.append((cGD, gd_sub[si]))
                    for kc in range(NCH):
                        tok = P.do(PE, lambda e, kc=kc, hc=hc, tcg=tcg, grp=grp,
                                   bank=bank: e.matmul(
                            bank[:],
                            grp[0][:, kc * 512 + hc * 128:kc * 512 + hc * 128 + 128],
                            xwc(kc, tcg),
                            start=(kc == 0), stop=(kc == NCH - 1)),
                            inc=cPE if kc == NCH - 1 else None,
                            waits=w6 if kc == 0 else ())
                    se_sil[(si, tcg, hc)] = P.do(SE, lambda e, hc=hc, bank=bank:
                                                 e.activation(
                        sil[:, hc * TC:(hc + 1) * TC], bank[:], AF.Silu),
                        inc=cSE, waits=[(cPE, tok)])
                for hc in range(4):
                    bank = pb[2 + hc % 2]
                    w7 = []
                    key = (si, tcg, hc - 2) if hc >= 2 else \
                        ((si, tcg - 1, 2 + hc) if tcg >= 1 else
                         (si - 1, NTC - 1, 2 + hc))
                    if key in ve_h2:
                        w7.append((cVE, ve_h2[key]))
                    for kc in range(NCH):
                        tok = P.do(PE, lambda e, kc=kc, hc=hc, tcg=tcg, grp=grp,
                                   bank=bank: e.matmul(
                            bank[:],
                            grp[1][:, kc * 512 + hc * 128:kc * 512 + hc * 128 + 128],
                            xwc(kc, tcg),
                            start=(kc == 0), stop=(kc == NCH - 1)),
                            inc=cPE if kc == NCH - 1 else None,
                            waits=w7 if kc == 0 else ())
                    w8 = [(cPE, tok), (cSE, se_sil[(si, tcg, hc)])]
                    ko = (si - 1, tcg) if si >= 1 else None
                    if ko in se_out:
                        w8.append((cSE, se_out[ko]))
                    tfirst = P.do(VE, lambda e, hc=hc, bank=bank:
                                  e.tensor_tensor(
                        inter[:, hc * TC:(hc + 1) * TC],
                        sil[:, hc * TC:(hc + 1) * TC], bank[:], ALU.mult),
                        waits=w8, inc=cVE)
                    ve_h2[(si, tcg, hc)] = tfirst
                    if kind == "e":
                        w9 = [(cVE, ve_wr[(si, tcg)])] if hc == 0 else []
                        P.do(VE, lambda e, hc=hc, wr=w_rep[tcg % 2]:
                             e.tensor_mul(
                            inter[:, hc * TC:(hc + 1) * TC],
                            inter[:, hc * TC:(hc + 1) * TC], wr[:, :]),
                            inc=cVE if hc == 3 else None,
                            waits=w9, drain=True)
                ve_inter = cVE.n
                se_last = None
                for dc in range(NCH):
                    bank = pb[5 + dc % 2]
                    w10 = [(cVE, ve_inter)] if dc == 0 else []
                    if dc == 0 and gd_acc:
                        w10.append((cGD, gd_acc[-1]))
                    if dc < 2:
                        ko2 = (si, tcg - 1) if tcg >= 1 else (si - 1, NTC - 1)
                        if ko2 in se_out:
                            w10.append((cSE, se_out[ko2]))
                    if dc >= 2:
                        w10.append((cSE, se_last_toks[dc - 2]))
                    for kc in range(4):
                        tok = P.do(PE, lambda e, kc=kc, dc=dc, grp=grp,
                                   bank=bank: e.matmul(
                            bank[:],
                            grp[2][:, kc * 1024 + dc * 128:kc * 1024 + dc * 128 + 128],
                            inter[:, kc * TC:(kc + 1) * TC],
                            start=(kc == 0), stop=(kc == 3)),
                            inc=cPE if kc == 3 else None,
                            waits=w10 if kc == 0 else ())
                    if dc == 0:
                        se_last_toks = []
                    se_last_toks.append(P.do(SE, lambda e, dc=dc, bank=bank:
                                             e.copy(
                        outsb[:, dc * TC:(dc + 1) * TC], bank[:]),
                        inc=cSE, waits=[(cPE, tok)]))
                se_out[(si, tcg)] = se_last_toks[-1]
                pe_last_here = cPE.n
                gd_acc.append(P.do(PO, lambda e, tcg=tcg, first=(si == 0):
                                   e.dma_start(
                    out=oac_ap(tcg), in_=outsb.ap([TC, NCH], [1, TC]),
                    accum_op=(ALU.bypass if first else ALU.add)),
                    inc=(cGD, 16), waits=[(cSE, se_out[(si, tcg)])]))
            pe_sub_last.append(cPE.n)

        sd_oi = []
        sd_xi = []
        ve_add = []
        sd_out = []
        for tcg in range(NTC):
            roll = xt16[0]
            w = [(cGD, gd_acc[-1])]
            if sd_out:
                w.append((cSD, sd_out[-1]))
            sd_oi.append(P.do(SP, lambda e, tcg=tcg: e.dma_start(
                out=outsb.ap([TC, NCH], [1, TC]), in_=oac_ap(tcg)),
                inc=(cSD, 16), waits=w))
            w = [(cVE, ve_add[tcg - 1])] if tcg >= 1 else \
                [(cVE, ve_xln[NTC - 1])]
            sd_xi.append(P.do(SP, lambda e, tcg=tcg, roll=roll: e.dma_start(
                out=roll.ap([TC, NCH], [1, TC]), in_=xsp_ap(tcg)),
                inc=(cSD, 16), waits=w))
            tok = None
            for c in range(NCH):
                w11 = [(cSD, sd_oi[tcg]), (cSD, sd_xi[tcg])] if c == 0 else []
                if c == 0 and tcg == 0:
                    w11.append((cSE, se_out[(len(subs) - 1, NTC - 1)]))
                if inv == 0:
                    tok = P.do(VE, lambda e, c=c, roll=roll: e.tensor_add(
                        sq[:, c * TC:(c + 1) * TC],
                        roll[:, c * TC:(c + 1) * TC],
                        outsb[:, c * TC:(c + 1) * TC]),
                        inc=cVE if c == NCH - 1 else None, waits=w11)
                else:
                    tok = P.do(VE, lambda e, c=c, roll=roll: e.tensor_add(
                        outsb[:, c * TC:(c + 1) * TC],
                        roll[:, c * TC:(c + 1) * TC],
                        outsb[:, c * TC:(c + 1) * TC]),
                        inc=cVE if c == NCH - 1 else None, waits=w11)
            ve_add.append(tok)
            if inv == 0:
                sd_out.append(P.do(SP, lambda e, tcg=tcg: e.dma_start(
                    out=xsp_ap(tcg), in_=sq.ap([TC, NCH], [1, TC])),
                    inc=(cSD, 16), waits=[(cVE, tok)]))
            else:
                bq_, tq = divmod(tcg, NTC_B)
                sd_out.append(P.do(SP, lambda e, bq_=bq_, tq=tq: e.dma_start(
                    out=out_d.rearrange("c p b n -> p b c n")[
                        :, bq_, :, tq * TC:(tq + 1) * TC],
                    in_=outsb.ap([TC, NCH], [1, TC])),
                    inc=(cSD, 16), waits=[(cVE, tok)]))

        play(P)
        abf.free(xw, xt16[0], sq, nmrep, rsrep, ttmp, sil, inter,
                 w_cm, wt16, w_rep[0], w_rep[1],
                 *[b_ for gi in range(2) for b_ in wg[gi]])
        af32.free(outsb, rows, gt)

    def phase_ssm():
        P = newplan()
        winb = abf.alloc("winb", NCH * 4096)
        xt16 = [abf.alloc(f"xt16s{i}", NCH * TC) for i in range(2)]
        xln3 = abf.alloc("xln3", NCH * TC)
        sq = abf.alloc("sqS", NCH * TC)
        nmrep = abf.alloc("nmrepS", TC)
        rsrep = abf.alloc("rsrepS", TC)
        ttmp = abf.alloc("ttmpS", TC)
        xpb = abf.alloc("xpb", 16 * (TC + 3) + 32)
        xcb = abf.alloc("xcb", 16 * TC)
        zb = abf.alloc("zb", 16 * TC)
        cacc = af32.alloc("cacc", 2 * TC)
        rows = af32.alloc("rowsS", 4 * TC)

        gd_win = P.do(PO, lambda e: e.dma_start(
            out=winb.ap([4096, NCH], [1, 4096]),
            in_=wdr["W_inT"].rearrange("(c p) n -> p c n", p=128)),
            inc=(cGD, 16))
        sd_x = []
        se_sq = []
        pe_ln = []
        ve_var = []
        se_rstd = []
        pe_rep = []
        ve_xln = []
        ve_conv = []
        se_z = {}
        se_xc = {}
        sd_sp = []
        pe_xz = {}
        for b in range(NB):
            P.do(VE, lambda e, b=b: e.memset(
                xpb.ap([TC + 3, 16], [1, 3]), 0.0), drain=True)
            for tq in range(NTC_B):
                ti = b * NTC_B + tq
                tcg = b * NTC_B + tq
                roll = xt16[ti % 2]
                w = []
                if ti >= 2:
                    w.append((cVE, ve_xln[ti - 2]))
                sd_x.append(P.do(SP, lambda e, tcg=tcg, roll=roll: e.dma_start(
                    out=roll.ap([TC, NCH], [1, TC]), in_=xsp_ap(tcg)),
                    inc=(cSD, 16), waits=w))
                w = [(cSD, sd_x[ti])]
                if ti >= 1:
                    w.append((cPE, pe_ln[ti - 1]))
                tok = None
                for c in range(NCH):
                    tok = P.do(SE, lambda e, c=c, roll=roll: e.activation(
                        sq[:, c * TC:(c + 1) * TC], roll[:, c * TC:(c + 1) * TC],
                        AF.Square), inc=cSE if c == NCH - 1 else None,
                        waits=w if c == 0 else ())
                se_sq.append(tok)
                w = [(cSD, sd_x[ti])]
                if ti >= 1:
                    w.append((cVE, ve_var[ti - 1]))
                for c in range(NCH):
                    P.do(PE, lambda e, c=c, roll=roll: e.matmul(
                        pb[0][0:1, :], ones_b[:, 0:1],
                        roll[:, c * TC:(c + 1) * TC],
                        start=(c == 0), stop=(c == NCH - 1)),
                        waits=w if c == 0 else ())
                for c in range(NCH):
                    tok = P.do(PE, lambda e, c=c: e.matmul(
                        pb[1][0:1, :], ones_b[:, 0:1], sq[:, c * TC:(c + 1) * TC],
                        start=(c == 0), stop=(c == NCH - 1)),
                        inc=cPE if c == NCH - 1 else None,
                        waits=[(cSE, se_sq[ti])] if c == 0 else ())
                pe_ln.append(tok)
                P.do(VE, lambda e: e.tensor_scalar_mul(
                    rows[0:1, 0:TC], pb[0][0:1, :], -1.0 / DM),
                    waits=[(cPE, pe_ln[ti])])
                P.do(VE, lambda e: e.tensor_scalar_mul(
                    rows[0:1, TC:2 * TC], pb[1][0:1, :], 1.0 / DM))
                P.do(VE, lambda e: e.tensor_mul(
                    rows[0:1, 2 * TC:3 * TC], rows[0:1, 0:TC], rows[0:1, 0:TC]),
                    drain=True)
                ve_var.append(P.do(VE, lambda e: e.tensor_sub(
                    rows[0:1, 2 * TC:3 * TC], rows[0:1, TC:2 * TC],
                    rows[0:1, 2 * TC:3 * TC]), inc=cVE, drain=True))
                P.do(SE, lambda e: e.activation(
                    rows[0:1, 2 * TC:3 * TC], rows[0:1, 2 * TC:3 * TC], AF.Ln,
                    bias=eps_s[0:1, 0:1]), waits=[(cVE, ve_var[ti])])
                se_rstd.append(P.do(SE, lambda e: e.activation(
                    rows[0:1, 3 * TC:4 * TC], rows[0:1, 2 * TC:3 * TC], AF.Exp,
                    scale=-0.5), inc=cSE, drain=True))
                wrep_ = [(cSE, se_rstd[ti])]
                if ti >= 1:
                    wrep_.append((cVE, ve_xln[ti - 1]))
                P.do(PE, lambda e: e.matmul(
                    pb[2][:], ones_f[0:1, :], rows[0:1, 0:TC],
                    start=True, stop=True), waits=wrep_)
                pe_rep.append(P.do(PE, lambda e: e.matmul(
                    pb[3][:], ones_f[0:1, :], rows[0:1, 3 * TC:4 * TC],
                    start=True, stop=True), inc=cPE))
                P.do(VE, lambda e: e.tensor_copy(nmrep[:, :], pb[2][:]),
                     waits=[(cPE, pe_rep[ti])])
                P.do(VE, lambda e: e.tensor_copy(rsrep[:, :], pb[3][:]))
                for c in range(NCH):
                    P.do(VE, lambda e, c=c, roll=roll: e.tensor_add(
                        ttmp[:, :], roll[:, c * TC:(c + 1) * TC], nmrep[:, :]),
                        drain=True)
                    tok = P.do(VE, lambda e, c=c: e.tensor_mul(
                        xln3[:, c * TC:(c + 1) * TC], ttmp[:, :], rsrep[:, :]),
                        inc=cVE if c == NCH - 1 else None, drain=True)
                ve_xln.append(tok)
                for ec in range(32):
                    bank = pb[4 + ec % 2]
                    w = [(cVE, ve_xln[ti])] if ec == 0 else []
                    if ec == 0 and ti == 0:
                        w.append((cGD, gd_win))
                    prev = (ti, ec - 2) if ec >= 2 else \
                        ((ti - 1, 30 + ec) if ti >= 1 else None)
                    if prev in se_z:
                        w.append((cSE, se_z[prev]))
                    if prev in se_xc:
                        w.append((cVE if False else cSE, se_xc[prev]))
                    for kc in range(NCH):
                        tok = P.do(PE, lambda e, kc=kc, ec=ec, bank=bank:
                                   e.matmul(
                            bank[:],
                            winb[:, kc * 4096 + ec * 128:kc * 4096 + ec * 128 + 128],
                            xln3[:, kc * TC:(kc + 1) * TC],
                            start=(kc == 0), stop=(kc == NCH - 1)),
                            inc=cPE if kc == NCH - 1 else None,
                            waits=w if kc == 0 else ())
                    pe_xz[(ti, ec)] = tok
                    if ec >= 16:
                        wz = [(cPE, tok)]
                        if ec == 16 and ti >= 1:
                            wz.append((cSD, sd_sp[2 * ti - 2]))
                        se_z[(ti, ec)] = P.do(SE, lambda e, ec=ec, bank=bank:
                                              e.copy(
                            zb[:, (ec - 16) * TC:(ec - 15) * TC], bank[:]),
                            inc=cSE, waits=wz)
                    else:
                        wxc = [(cPE, tok)]
                        if ec == 0 and ti >= 1:
                            wxc.append((cSD, sd_sp[2 * ti - 1]))
                        se_xc[(ti, ec)] = P.do(SE, lambda e, ec=ec, bank=bank:
                                               e.copy(
                            bass.AP(abf_t, xpb.off + ec * (TC + 3) + 3,
                                    [[ABF, 128], [1, TC]]), bank[:]),
                            inc=cSE, waits=wxc)
                        woff = xpb.off + ec * (TC + 3)
                        w12 = [(cSE, se_xc[(ti, ec)])]
                        pec = (ti, ec - 1) if ec >= 1 else \
                            ((ti - 1, 15) if ti >= 1 else None)
                        if pec in se_xc:
                            w12.append((cSE, se_xc[pec]))
                        P.do(VE, lambda e, woff=woff, ec=ec: e.tensor_single_scalar(
                            cacc[:, 0:TC],
                            bass.AP(abf_t, woff, [[ABF, 128], [1, TC]]),
                            cw_s[:, ec * 4:ec * 4 + 1], ALU.mult), waits=w12)
                        for jj in range(1, 4):
                            P.do(VE, lambda e, woff=woff, ec=ec, jj=jj:
                                 e.scalar_tensor_tensor(
                                cacc[:, 0:TC],
                                bass.AP(abf_t, woff + jj, [[ABF, 128], [1, TC]]),
                                cw_s[:, ec * 4 + jj:ec * 4 + jj + 1],
                                cacc[:, 0:TC], ALU.mult, ALU.add), drain=True)
                        tok2 = P.do(VE, lambda e, woff=woff: e.tensor_copy(
                            bass.AP(abf_t, woff, [[ABF, 128], [1, 3]]),
                            bass.AP(abf_t, woff + TC, [[ABF, 128], [1, 3]])),
                            inc=cVE, drain=True)
                        ve_conv.append(tok2)
                        se_xc[(ti, ec)] = P.do(SE, lambda e, ec=ec: e.activation(
                            xcb[:, ec * TC:(ec + 1) * TC], cacc[:, 0:TC],
                            AF.Silu, bias=cb_s[:, ec:ec + 1]), inc=cSE,
                            waits=[(cVE, tok2)])
                w13 = [(cSE, se_z[(ti, 31)])]
                if len(sd_sp) >= 2:
                    w13.append((cSD, sd_sp[-2]))
                sd_sp.append(P.do(SP, lambda e, b=b, tq=tq: e.dma_start(
                    out=z_d[b, :, :, tq * TC:(tq + 1) * TC],
                    in_=zb.ap([TC, 16], [1, TC])), inc=(cSD, 16), waits=w13))
                sd_sp.append(P.do(SP, lambda e, b=b, tq=tq: e.dma_start(
                    out=xc_d[b, :, :, tq * TC:(tq + 1) * TC],
                    in_=xcb.ap([TC, 16], [1, TC])), inc=(cSD, 16),
                    waits=[(cSE, se_xc[(ti, 15)])]))
        play(P)
        abf.free(winb, xt16[0], xt16[1], xln3, sq, nmrep, rsrep, ttmp,
                 xpb, xcb, zb)
        af32.free(cacc, rows)

        P = newplan()
        xcb2 = [abf.alloc("xcb2A", 16 * TC)]
        xcb2.append(xcb2[0])
        dbrep = abf.alloc("dbrep", 16 * TC)
        crep = abf.alloc("crep", 16 * TC)
        hbuf = abf.alloc("hbuf", 16 * TC)
        yall = abf.alloc("yall", 16 * TC)
        zb2 = abf.alloc("zb2", 16 * TC)
        zs16 = abf.alloc("zs16", TC)
        woutb = abf.alloc("woutb", 16 * 1024)
        w16 = [abf.alloc(f"w16{i}", TC) for i in range(4)]
        bcsb = abf.alloc("bcsb", TC)
        xres = abf.alloc("xres", NCH * TC)
        state = af32.alloc("state", 256)
        da16 = [af32.alloc(f"daf{i}", TC) for i in range(4)]
        bcf = af32.alloc("bcf", 2 * TC)
        drep = af32.alloc("drep", TC)
        ytmp = af32.alloc("ytmp", TC)

        gd_wout = P.do(PO, lambda e: e.dma_start(
            out=woutb.ap([1024, 16], [1, 1024]),
            in_=wdr["W_outT"].rearrange("(c p) n -> p c n", p=128)),
            inc=(cGD, 16))

        sd_xc = []
        sd_z = []
        sd_xr = []
        sd_xo = []
        ve_fin = []
        for b in range(NB):
            P.do(VE, lambda e: e.memset(state[:, :], 0.0))
            ve_st0 = P.do(VE, lambda e: e.engine_nop(), inc=cVE)
            for tq in range(NTC_B):
                ti = b * NTC_B + tq
                tcg = b * NTC_B + tq
                xc2 = xcb2[0]
                w = []
                if ti >= 1:
                    w.append((cVE, ve_fin[ti - 1][0]))
                sd_xc.append(P.do(SP, lambda e, b=b, tq=tq, xc2=xc2:
                                  e.dma_start(
                    out=xc2.ap([TC, 16], [1, TC]),
                    in_=xc_d[b, :, :, tq * TC:(tq + 1) * TC]),
                    inc=(cSD, 16), waits=w))
                w = []
                if ti >= 1:
                    w.append((cVE, ve_fin[ti - 1][1]))
                sd_z.append(P.do(SP, lambda e, b=b, tq=tq: e.dma_start(
                    out=zb2.ap([TC, 16], [1, TC]),
                    in_=z_d[b, :, :, tq * TC:(tq + 1) * TC]),
                    inc=(cSD, 16), waits=w))
                w = []
                if ti >= 1:
                    w.append((cVE, ve_fin[ti - 1][2]))
                    w.append((cSD, sd_xo[-1]))
                sd_xr.append(P.do(SP, lambda e, tcg=tcg: e.dma_start(
                    out=xres.ap([TC, NCH], [1, TC]), in_=xsp_ap(tcg)),
                    inc=(cSD, 16), waits=w))
                w = [(cSD, sd_xc[ti])]
                if ti >= 1:
                    w.append((cVE, ve_bc))
                for kc in range(16):
                    tok = P.do(PE, lambda e, kc=kc, xc2=xc2: e.matmul(
                        pb[0][0:80, :], wx_s[:, kc * 80:(kc + 1) * 80],
                        xc2[:, kc * TC:(kc + 1) * TC],
                        start=(kc == 0), stop=(kc == 15)),
                        inc=cPE if kc == 15 else None,
                        waits=w if kc == 0 else ())
                ve_wx = P.do(VE, lambda e: e.tensor_copy(
                    bcf[0:80, 0:TC], pb[0][0:80, :]), inc=cVE,
                    waits=[(cPE, tok)])
                se_d = P.do(SE, lambda e: e.activation(
                    bcf[0:1, TC:2 * TC], bcf[0:1, 0:TC], AF.Abs),
                    waits=[(cVE, ve_wx)], drain=True)
                P.do(SE, lambda e: e.activation(
                    bcf[0:1, TC:2 * TC], bcf[0:1, TC:2 * TC], AF.Exp,
                    scale=-1.0), drain=True)
                se_d = P.do(SE, lambda e: e.activation(
                    bcf[0:1, TC:2 * TC], bcf[0:1, TC:2 * TC], AF.Ln, bias=1.0),
                    inc=cSE, drain=True)
                P.do(VE, lambda e: e.tensor_scalar_max(
                    drep[0:1, 0:TC], bcf[0:1, 0:TC], 0.0),
                    waits=[(cSE, se_d)], drain=True)
                ve_dl = P.do(VE, lambda e: e.tensor_add(
                    drep[0:1, 0:TC], drep[0:1, 0:TC], bcf[0:1, TC:2 * TC]),
                    inc=cVE, drain=True)
                pe_dr = P.do(PE, lambda e: e.matmul(
                    pb[1][:], ones_f[0:1, :], drep[0:1, 0:TC],
                    start=True, stop=True), inc=cPE, waits=[(cVE, ve_dl)])
                ve_drep = P.do(VE, lambda e: e.tensor_copy(
                    drep[:, :], pb[1][:]), inc=cVE, waits=[(cPE, pe_dr)])
                P.do(VE, lambda e: e.tensor_mul(
                    bcsb[32:48, 0:TC], bcf[32:48, 0:TC], drep[32:48, 0:TC]),
                    waits=[(cVE, ve_drep)], drain=True)
                ve_bc = P.do(VE, lambda e: e.tensor_copy(
                    bcsb[64:80, 0:TC], bcf[64:80, 0:TC]), inc=cVE, drain=True)
                pe_rep_t = []
                for s in range(16):
                    w14 = [(cVE, ve_bc)] if s == 0 else []
                    if len(pe_rep_t) >= 2:
                        w14.append((cVE, ve_reps[s - 2]))
                    bank = pb[2 + s % 2]
                    ptok = P.do(PE, lambda e, s=s, bank=bank: e.matmul(
                        bank[:], sel16_s[32:48, s * 128:(s + 1) * 128],
                        bcsb[32:48, 0:TC], start=True, stop=True),
                        inc=cPE, waits=w14)
                    pe_rep_t.append(ptok)
                    if s == 0:
                        ve_reps = []
                    ve_reps.append(P.do(VE, lambda e, s=s, bank=bank:
                                        e.tensor_copy(
                        dbrep[:, s * TC:(s + 1) * TC], bank[:]), inc=cVE,
                        waits=[(cPE, ptok)]))
                for s in range(16):
                    w14 = []
                    if len(pe_rep_t) >= 2:
                        w14.append((cVE, ve_reps[14 + s]))
                    bank = pb[2 + s % 2]
                    ptok = P.do(PE, lambda e, s=s, bank=bank: e.matmul(
                        bank[:], sel16_s[64:80, s * 128:(s + 1) * 128],
                        bcsb[64:80, 0:TC], start=True, stop=True),
                        inc=cPE, waits=w14)
                    pe_rep_t.append(ptok)
                    ve_reps.append(P.do(VE, lambda e, s=s, bank=bank:
                                        e.tensor_copy(
                        crep[:, s * TC:(s + 1) * TC], bank[:]), inc=cVE,
                        waits=[(cPE, ptok)]))
                ve_crep_done = ve_reps[-1]
                se_da = {}
                ve_scan = {}
                po_w = {}
                for dchunk in range(16):
                    for s in range(16):
                        ii = dchunk * 16 + s
                        par = ii % 4
                        w15 = [(cVE, ve_drep)] if ii == 0 else []
                        if ii >= 4:
                            w15.append((cVE, ve_scan.get(ii - 4, 0)))
                        se_da[ii] = P.do(SE, lambda e, dchunk=dchunk, s=s,
                                         par=par: e.activation(
                            da16[par][:, :], drep[:, :], AF.Exp,
                            scale=acol_s[:, dchunk * 16 + s:dchunk * 16 + s + 1]),
                            inc=cSE, waits=w15)
                        w16_ = [(cVE, ve_crep_done)] if ii == 0 else []
                        if ii >= 4:
                            w16_.append((cVE, ve_scan.get(ii - 4, 0)))
                        po_w[ii] = P.do(PO, lambda e, dchunk=dchunk, s=s,
                                        par=par, xc2=xc2: e.tensor_tensor(
                            w16[par][:, :], xc2[:, dchunk * TC:(dchunk + 1) * TC],
                            dbrep[:, s * TC:(s + 1) * TC], ALU.mult),
                            inc=cPO, waits=w16_)
                        w17 = [(cSE, se_da[ii]), (cPO, po_w[ii])]
                        if ii == 0:
                            w17.append((cVE, ve_st0))
                        P.do(VE, lambda e, s=s, par=par, dchunk=dchunk:
                             e.tensor_tensor_scan(
                            hbuf[:, s * TC:(s + 1) * TC], da16[par][:, :],
                            w16[par][:, :],
                            state[:, dchunk * 16 + s:dchunk * 16 + s + 1],
                            ALU.mult, ALU.add), waits=w17)
                        ve_scan[ii] = P.do(VE, lambda e, s=s, dchunk=dchunk:
                                           e.tensor_copy(
                            state[:, dchunk * 16 + s:dchunk * 16 + s + 1],
                            bass.AP(abf_t, hbuf.off + s * TC + TC - 1,
                                    [[ABF, 128], [1, 1]])), inc=cVE,
                            drain=True)
                    P.do(VE, lambda e: e.tensor_mul(
                        hbuf[:, :], hbuf[:, :], crep[:, :]), drain=True)
                    P.do(VE, lambda e: e.tensor_reduce(
                        ytmp[:, 0:TC],
                        bass.AP(abf_t, hbuf.off,
                                [[ABF, 128], [1, TC], [TC, 16]]),
                        AX.X, ALU.add), drain=True)
                    w18 = [(cSD, sd_z[ti])]
                    if dchunk >= 1:
                        w18.append((cVE, ve_scan[dchunk * 16 - 1]))
                    se_zs = P.do(SE, lambda e, dchunk=dchunk: e.activation(
                        zs16[:, :], zb2[:, dchunk * TC:(dchunk + 1) * TC],
                        AF.Silu), inc=cSE, waits=w18)
                    tok = P.do(VE, lambda e, dchunk=dchunk: e.tensor_mul(
                        yall[:, dchunk * TC:(dchunk + 1) * TC], ytmp[:, 0:TC],
                        zs16[:, :]), inc=cVE, waits=[(cSE, se_zs)], drain=True)
                    ve_scan[dchunk * 16 + 15] = tok
                ve_y = tok
                toks_fin = []
                for fc in range(NCH):
                    bank = pb[4 + fc % 2]
                    w19 = [(cVE, ve_y), (cSD, sd_xr[ti])]
                    if fc == 0 and ti == 0:
                        w19.append((cGD, gd_wout))
                    if fc >= 2:
                        w19.append((cVE, toks_fin[fc - 2]))
                    for kc in range(16):
                        tok = P.do(PE, lambda e, kc=kc, fc=fc, bank=bank:
                                   e.matmul(
                            bank[:],
                            woutb[:, kc * 1024 + fc * 128:kc * 1024 + fc * 128 + 128],
                            yall[:, kc * TC:(kc + 1) * TC],
                            start=(kc == 0), stop=(kc == 15)),
                            inc=cPE if kc == 15 else None,
                            waits=w19 if kc == 0 else ())
                    toks_fin.append(P.do(VE, lambda e, fc=fc, bank=bank:
                                         e.tensor_add(
                        xres[:, fc * TC:(fc + 1) * TC],
                        xres[:, fc * TC:(fc + 1) * TC], bank[:]),
                        inc=cVE, waits=[(cPE, tok)]))
                w20 = [(cVE, toks_fin[-1])]
                if len(sd_xo) >= 1:
                    w20.append((cSD, sd_xo[-1]))
                sd_xo.append(P.do(SP, lambda e, tcg=tcg: e.dma_start(
                    out=xsp_ap(tcg), in_=xres.ap([TC, NCH], [1, TC])),
                    inc=(cSD, 16), waits=w20))
                ve_fin.append((ve_scan[255], cVE.n, toks_fin[-1]))
        play(P)
        abf.free(xcb2[0], dbrep, crep, hbuf, yall, zb2, zs16, woutb,
                 w16[0], w16[1], w16[2], w16[3], bcsb, xres)
        af32.free(state, bcf, drep, ytmp, da16[0], da16[1], da16[2], da16[3])

    if upto in ("ffn1", "ssm", "full"):
        phase_ffn(0)
    if upto in ("ssm", "full"):
        phase_ssm()
    if upto == "full":
        phase_ffn(1)

    if upto == "full":
        P = None
    else:
        P = newplan()
    ostg = af32.alloc("ostg", NCH * TC) if P else None
    o16 = abf.alloc("o16", NCH * TC) if P else None
    sd_fin = []
    sd_ld = []
    ve_o = []
    for tcg in range(NTC if P else 0):
        bq_, tq = divmod(tcg, NTC_B)
        w = [(cVE, ve_o[tcg - 1])] if tcg >= 1 else []
        sd_ld.append(P.do(SP, lambda e, bq_=bq_, tq=tq: e.dma_start(
            out=o16.ap([TC, NCH], [1, TC]),
            in_=bass.AP(xsp.tensor, bq_ * S + tq * TC,
                        [[NCH * T, 128], [T, NCH], [1, TC]])),
            inc=(cSD, 16), waits=w))
        w = [(cSD, sd_ld[tcg])]
        if tcg >= 2:
            w.append((cSD, sd_fin[tcg - 2]))
        tok = None
        for c in range(NCH):
            tok = P.do(VE, lambda e, c=c: e.tensor_copy(
                ostg[:, c * TC:(c + 1) * TC], o16[:, c * TC:(c + 1) * TC]),
                inc=cVE if c == NCH - 1 else None, waits=w if c == 0 else ())
        ve_o.append(tok)
        sd_fin.append(P.do(SP, lambda e, bq_=bq_, tq=tq: e.dma_start(
            out=out_d.rearrange("c p b n -> p b c n")[:, bq_, :,
                                                      tq * TC:(tq + 1) * TC],
            in_=ostg.ap([TC, NCH], [1, TC])),
            inc=(cSD, 16), waits=[(cVE, tok)]))
    if P:
        play(P)
        af32.free(ostg)
        abf.free(o16)

    Pf = newplan()
    fin_sd, fin_gd = cSD.n, cGD.n
    Pf.do(SP, lambda e: e.nop(), waits=[(cSD, fin_sd)])
    Pf.do(PO, lambda e: e.nop(), waits=[(cGD, fin_gd)])
    Pf.play(nc, None)

    g.close()
    return nc


def _assemble(res):
    outs = []
    for i in range(8):
        a = np.asarray(res.results[i]["xout"], dtype=np.float32)
        outs.append(a.reshape(DM, NB, S).transpose(1, 2, 0))
    return np.concatenate(outs, axis=0)


def kernel(**inputs):
    import os
    upto = os.environ.get("KUPTO", "full")
    if "nc" not in _CACHE:
        _CACHE["nc"] = build_nc(upto=upto)
    nc = _CACHE["nc"]
    host = _host_layout(inputs)
    consts = _host_constants()
    in_maps = []
    for i in range(8):
        m = {"xin": host["_x_percore"][i]}
        for nm in ("wq", "wk", "wv", "wo", "wfq", "wfg", "sw1", "sw2", "sw3",
                   "gate_w", "ew1", "ew2", "ew3", "W_inT", "W_outT", "W_xTp",
                   "conv_w", "conv_b", "A_log", "bq", "bk", "bv", "bo", "bfq",
                   "bfg", "U_pz"):
            m[nm] = host[nm]
        m.update(consts)
        in_maps.append(m)
    res = run_bass_kernel_spmd(nc, in_maps, core_ids=list(range(8)))
    return _assemble(res)
